# version 01 (final)
# baseline (speedup 1.0000x reference)
import contextlib
import numpy as np

import concourse.bass as bass
import concourse.mybir as mybir
from concourse.bass_utils import run_bass_kernel_spmd

F32 = mybir.dt.float32
BF16 = mybir.dt.bfloat16
AF = mybir.ActivationFunctionType
ALU = mybir.AluOpType
AX = mybir.AxisListType

ENGS = ("sync", "tensor", "vector", "scalar", "gpsimd")

NCORES = 8
D = 1024
DC = D // 128
TT = 512
TOK = 4096
NT = TOK // TT
LN_EPS = 1e-5


class P:

    def __init__(self, nc):
        self.nc = nc
        self.cnt = {"sld": 0, "sgp": 0, "ssc": 0, "spe": 0, "sdv": 0,
                    "sac": 0}
        self.seq = {e: [] for e in ENGS}

    def emit(self, eng, method, args=(), kwargs=None, waits=(), inc=None,
             inc_amt=1):
        tok = None
        if inc is not None:
            self.cnt[inc] += inc_amt
            tok = (inc, self.cnt[inc], eng)
        waits = tuple(w for w in waits if w is not None)
        self.seq[eng].append((waits, method, args, dict(kwargs or {}), inc,
                              inc_amt, tok))
        return tok

    DMA_SEMS = {"sync": "sld", "gpsimd": "sgp", "scalar": "ssc"}

    def dma(self, qeng, out, in_, waits=(), **kw):
        return self.emit(qeng, "dma_start", (), {"out": out, "in_": in_, **kw},
                         waits, self.DMA_SEMS[qeng], 16)

    def mm(self, out, lhsT, rhs, start, stop, waits=()):
        return self.emit("tensor", "matmul", (out, lhsT, rhs),
                         {"start": start, "stop": stop}, waits,
                         "spe" if stop else None)

    def tr(self, out, in_, ident, waits=()):
        return self.emit("tensor", "transpose", (out, in_, ident), None,
                         waits, "spe")

    def dve(self, method, *args, waits=(), **kw):
        return self.emit("vector", method, args, kw, waits, "sdv")

    def act(self, method, *args, waits=(), **kw):
        return self.emit("scalar", method, args, kw, waits, "sac")

    def barrier_end(self, waits):
        self.emit("gpsimd", "engine_nop", (), None, waits=waits)

    def finalize(self):
        nc = self.nc
        with contextlib.ExitStack() as es:
            sems = {n: es.enter_context(nc.semaphore(n)) for n in self.cnt}
            block = es.enter_context(nc.Block())
            for eng_name in ENGS:
                ops = self.seq[eng_name]
                if not ops:
                    continue
                chain = eng_name in ("vector", "scalar", "gpsimd")

                def make(eng_name, ops, chain):
                    chain_tok = []
                    prev = None
                    for rec in ops:
                        chain_tok.append(prev)
                        if rec[1] != "dma_start" and rec[6] is not None:
                            prev = rec[6]

                    def body(eng):
                        waited = {}
                        for i, (waits, method, args, kwargs, inc, amt,
                                tok) in enumerate(ops):
                            ws = list(waits)
                            if chain and method != "dma_start" and chain_tok[i]:
                                ws.append(chain_tok[i] + (True,))
                            for w in ws:
                                sname, val, peng = w[0], w[1], w[2]
                                force = len(w) > 3 and w[3]
                                if (not force and peng == eng_name
                                        and sname not in ("sld", "sgp", "ssc")):
                                    continue
                                if waited.get(sname, 0) >= val:
                                    continue
                                eng.wait_ge(sems[sname], val)
                                waited[sname] = val
                            inst = getattr(eng, method)(*args, **kwargs)
                            if inc is not None:
                                inst.then_inc(sems[inc], amt)
                    return body

                getattr(block, eng_name)(make(eng_name, ops, chain))
        return nc


def _run(nc, in_maps, out_names):
    res = run_bass_kernel_spmd(nc, in_maps, core_ids=list(range(NCORES)))
    return [{k: r[k] for k in out_names} for r in res.results]


def _f32(x):
    return np.ascontiguousarray(x, np.float32)



class Common:
    pass


def alloc_common(nc, es, ntok=TT):
    c = Common()
    sb = lambda name, shape, dt: es.enter_context(nc.sbuf_tensor(name, shape, dt))
    c.ntok = ntok
    c.xt = sb("c_xt", [128, DC * ntok], F32)
    c.xsq = sb("c_xsq", [128, DC * ntok], BF16)
    c.xn = sb("c_xn", [128, DC * ntok], BF16)
    c.t32 = sb("c_t32", [128, ntok], F32)
    c.rows = sb("c_rows", [1, 4 * ntok], F32)
    c.onescol_bf = sb("c_onescol_bf", [128, 1], BF16)
    c.onescol_f = sb("c_onescol_f", [128, 1], F32)
    c.onesrow_f = sb("c_onesrow_f", [1, 128], F32)
    c.onesrow_bf = sb("c_onesrow_bf", [1, max(128, ntok)], BF16)
    c.ident = sb("c_ident", [128, 128], F32)
    c.cc = sb("c_cc", [128, 8], F32)
    return c


CC_LNEPS = 0
CC_NHALF = 1
CC_NONE = 2
CC_E6 = 3
CC_P1 = 4
CC_ONE = 5


def emit_const_init(p, c, ident_dram):
    toks = [
        p.dve("memset", c.onescol_bf[:], 1.0),
        p.dve("memset", c.onescol_f[:], 1.0),
        p.dve("memset", c.onesrow_f[:], 1.0),
        p.dve("memset", c.onesrow_bf[:], 1.0),
        p.dma("sync", c.ident[:], ident_dram),
    ]
    for i, v in [(CC_LNEPS, LN_EPS), (CC_NHALF, -0.5), (CC_NONE, -1.0),
                 (CC_E6, 1e-6), (CC_P1, 0.1), (CC_ONE, 1.0)]:
        toks.append(p.dve("memset", c.cc[:, i:i + 1], v))
    c.cc_tok = toks[-1]
    return toks


def emit_ln(p, c, pb, x_dram, cols, xt_war=(), xn_war=(), pb_war=(),
            xn_ap_fn=None):
    ntok = c.ntok
    ld = []
    for dc in range(DC):
        ld.append(p.dma("sync", c.xt[:, dc * ntok:(dc + 1) * ntok],
                        x_dram[dc * 128:(dc + 1) * 128, cols],
                        waits=(list(xt_war) if dc == 0 else [])))
    sqs = []
    for dc in range(DC):
        s = slice(dc * ntok, (dc + 1) * ntok)
        sqs.append(p.act("activation", c.xsq[:, s], c.xt[:, s], AF.Square,
                         waits=[ld[dc]]))
    mm1 = mm2 = None
    for dc in range(DC):
        s = slice(dc * ntok, (dc + 1) * ntok)
        mm1 = p.mm(pb[0][0:1, 0:ntok], c.onescol_f[:], c.xt[:, s],
                   dc == 0, dc == DC - 1,
                   waits=[ld[dc]] + (list(pb_war) if dc == 0 else []))
        mm2 = p.mm(pb[1][0:1, 0:ntok], c.onescol_bf[:], c.xsq[:, s],
                   dc == 0, dc == DC - 1, waits=[sqs[dc]])
    n = ntok
    m_row = c.rows[0:1, 0:n]
    r_row = c.rows[0:1, n:2 * n]
    mr_row = c.rows[0:1, 2 * n:3 * n]
    w_row = c.rows[0:1, 3 * n:4 * n]
    tm = p.dve("tensor_scalar_mul", m_row, pb[0][0:1, 0:n], 1.0 / D, waits=[mm1])
    te = p.dve("tensor_scalar_mul", w_row, pb[1][0:1, 0:n], 1.0 / D,
               waits=[mm2])
    p.dve("scalar_tensor_tensor", r_row, m_row, -1.0, m_row, ALU.mult,
          ALU.mult, waits=[tm])
    tv = p.dve("tensor_tensor", mr_row, r_row, w_row, ALU.add, waits=[te])
    tl = p.act("activation", w_row, mr_row, AF.Ln,
               bias=c.cc[0:1, CC_LNEPS:CC_LNEPS + 1], scale=1.0,
               waits=[tv, c.cc_tok])
    tr_ = p.act("activation", r_row, w_row, AF.Exp,
                scale=c.cc[0:1, CC_NHALF:CC_NHALF + 1], waits=[tl])
    tmr = p.dve("tensor_tensor", mr_row, m_row, r_row, ALU.mult, waits=[tr_])
    b1 = p.mm(pb[2][:, 0:n], c.onesrow_f[:], r_row, True, True, waits=[tr_])
    b2 = p.mm(pb[3][:, 0:n], c.onesrow_f[:], mr_row, True, True, waits=[tmr])
    xn_tok = None
    for dc in range(DC):
        s = slice(dc * ntok, (dc + 1) * ntok)
        out_ap = c.xn[:, s] if xn_ap_fn is None else xn_ap_fn(dc)
        p.dve("tensor_tensor", c.t32[:], c.xt[:, s], pb[2][:, 0:n], ALU.mult,
              waits=[b1] + (list(xn_war) if dc == 0 else []))
        xn_tok = p.dve("tensor_tensor", out_ap, c.t32[:], pb[3][:, 0:n],
                       ALU.subtract, waits=[b2])
    return {"ld": ld, "xn": xn_tok}



def build_ffn0(half=0.5, tok=TOK, dbg=False):
    NTt = tok // TT
    DBG = "ExternalOutput" if dbg else "Internal"
    nc = bass.Bass("TRN2", target_bir_lowering=False, debug=False)
    x_d = nc.dram_tensor("x", [D, tok], F32, kind="ExternalInput").ap()
    wr_d = nc.dram_tensor("Wr", [D, 8], F32, kind="ExternalInput").ap()
    br_d = nc.dram_tensor("br", [1, 8], F32, kind="ExternalInput").ap()
    w1_d = nc.dram_tensor("W1", [8, D, 2048], F32, kind="ExternalInput").ap()
    b1c_d = nc.dram_tensor("b1c", [128, 128], F32, kind="ExternalInput").ap()
    w2_d = nc.dram_tensor("W2", [8, 2048, D], F32, kind="ExternalInput").ap()
    b2_d = nc.dram_tensor("B2", [8, D], F32, kind="ExternalInput").ap()
    esel_d = nc.dram_tensor("esel", [8, D], F32, kind="ExternalInput").ap()
    ident_d = nc.dram_tensor("ident", [128, 128], F32, kind="ExternalInput").ap()
    out_d = nc.dram_tensor("resid_out", [D, tok], F32,
                           kind="ExternalOutput").ap()

    xn_d = nc.dram_tensor("xn_scr", [D, tok], BF16, kind=DBG).ap()
    if dbg:
        dbg_lem = nc.dram_tensor("dbg_lem", [8, tok], F32, kind=DBG).ap()
        dbg_ltok = nc.dram_tensor("dbg_ltok", [128, (tok // TT) * 32], F32, kind=DBG).ap()
        dbg_rmax = nc.dram_tensor("dbg_rmax", [128, (tok // TT) * 4], F32, kind=DBG).ap()
        dbg_negr = nc.dram_tensor("dbg_negr", [128, (tok // TT) * 4], F32, kind=DBG).ap()
        dbg_etok = nc.dram_tensor("dbg_etok", [128, (tok // TT) * 32], F32, kind=DBG).ap()
        dbg_sume = nc.dram_tensor("dbg_sume", [128, (tok // TT) * 4], F32, kind=DBG).ap()
        dbg_mxp = nc.dram_tensor("dbg_mxp", [128, (tok // TT) * 4], F32, kind=DBG).ap()
        dbg_oh = nc.dram_tensor("dbg_oh", [128, (tok // TT) * 32], F32, kind=DBG).ap()
        dbg_ohst = nc.dram_tensor("dbg_ohst", [128, (tok // TT) * 32], F32, kind=DBG).ap()
    ohs_d = nc.dram_tensor("ohs_scr", [8, tok], F32, kind=DBG).ap()
    y_d = nc.dram_tensor("y_scr", [D, tok], F32, kind=DBG).ap()

    p = P(nc)
    with contextlib.ExitStack() as es:
        c = alloc_common(nc, es, TT)
        sb = lambda name, shape, dt: es.enter_context(
            nc.sbuf_tensor(name, shape, dt))
        pb = [es.enter_context(nc.psum_tensor(f"pb{i}", [128, TT], F32))
              for i in range(8)]
        wr_bf = sb("wr_bf", [128, 64], BF16)
        br_bf = sb("br_bf", [1, 8], BF16)
        lem = sb("lem", [8, TT], F32)
        ltok = sb("ltok", [128, 32], F32)
        rmax = sb("rmax", [128, 4], F32)
        negr = sb("negr", [128, 4], F32)
        etok = sb("etok", [128, 32], F32)
        sume = sb("sume", [128, 4], F32)
        mxp = sb("mxp", [128, 4], F32)
        oh = sb("oh", [128, 32], F32)
        ohs_tok = sb("ohs_tok", [128, 32], F32)
        ohs_em = sb("ohs_em", [8, TT], F32)
        esel_sb = sb("esel_sb", [8, D], F32)
        b1c_sb = sb("b1c_sb", [128, 128], F32)
        b2_bf = sb("b2_bf", [8, D], BF16)
        w1_bf = sb("w1_bf", [128, DC * 2048], BF16)
        w2_bf = sb("w2_bf", [128, 16 * D], BF16)
        wstage = sb("wstage", [128, 2048], F32)
        xnb = [sb(f"xnb{i}", [128, DC * TT], BF16) for i in range(2)]
        ohs_l = sb("ohs_l", [8, TT], F32)
        mask_bf = sb("mask_bf", [128, TT], BF16)
        h_sb = sb("h_sb", [128, 16 * TT], BF16)
        hstage = [sb(f"hstage{i}", [128, TT], F32) for i in range(2)]
        ystage = [sb(f"ystage{i}", [128, TT], F32) for i in range(2)]
        ybuf = sb("ybuf", [128, DC * TT], F32)
        ohs3 = sb("ohs3", [8, TT], F32)
        ohs3_bf = sb("ohs3_bf", [8, TT], BF16)
        resout = [sb(f"resout{i}", [128, TT], F32) for i in range(2)]

        ct = emit_const_init(p, c, ident_d)
        t_esel = p.dma("sync", esel_sb[:], esel_d)
        t_b1c = p.dma("sync", b1c_sb[:], b1c_d)
        t_b2 = p.dma("sync", wstage[0:8, 0:1024], b2_d)
        t_b2c = p.dve("tensor_copy", b2_bf[:], wstage[0:8, 0:1024],
                      waits=[t_b2])
        t_wr = p.dma("sync",
                     wstage[:, 1024:1088].rearrange("p (c e) -> p c e", e=8),
                     wr_d.rearrange("(c p) e -> p c e", p=128),
                     waits=[])
        t_wrc = p.dve("tensor_copy", wr_bf[:], wstage[:, 1024:1088],
                      waits=[t_wr])
        t_brl = p.dma("sync", wstage[0:1, 1088:1096], br_d, waits=[])
        t_brc = p.dve("tensor_copy", br_bf[:], wstage[0:1, 1088:1096],
                      waits=[t_brl])

        last_xn_store = None
        last_ohs_store = None
        prev = {}
        for ti in range(NTt):
            cols = slice(ti * TT, (ti + 1) * TT)
            ln = emit_ln(p, c, pb, x_d, cols,
                         xt_war=[prev.get("xn")],
                         xn_war=[prev.get("xn_store")],
                         pb_war=[prev.get("route_done")])
            st = None
            for dc in range(DC):
                st = p.dma("sync", xn_d[dc * 128:(dc + 1) * 128, cols],
                           c.xn[:, dc * TT:(dc + 1) * TT], waits=[ln["xn"]])
            last_xn_store = st
            mmr = None
            for dc in range(DC):
                w = []
                if dc == 0:
                    w = [ln["xn"], t_wrc, prev.get("route_done")]
                mmr = p.mm(pb[4][0:8, :], wr_bf[:, dc * 8:(dc + 1) * 8],
                           c.xn[:, dc * TT:(dc + 1) * TT], dc == 0, False,
                           waits=w)
            mmr = p.mm(pb[4][0:8, :], br_bf[:], c.onesrow_bf[0:1, 0:TT],
                       False, True, waits=[t_brc, ct[2]])
            t_lem = p.dve("tensor_copy", lem[:], pb[4][0:8, :], waits=[mmr])
            tt0 = None
            for g in range(4):
                w = [t_lem, ct[3]]
                if g == 0:
                    w.append(prev.get("route_done"))
                tt0 = p.tr(pb[5][:, g * 8:(g + 1) * 8],
                           lem[0:8, g * 128:(g + 1) * 128],
                           c.ident[0:8, 0:8], waits=w)
            t_ltok = p.dve("tensor_copy", ltok[:], pb[5][:, 0:32], waits=[tt0])
            t_rmax = p.dve("tensor_reduce", rmax[:],
                           ltok[:].rearrange("p (g e) -> p g e", e=8), AX.X,
                           ALU.max, waits=[t_ltok + (True,)])
            t_negr = p.dve("tensor_scalar_mul", negr[:], rmax[:], -1.0,
                           waits=[t_rmax])
            t_exp = []
            for g in range(4):
                t_exp.append(p.act("activation", etok[:, g * 8:(g + 1) * 8],
                                   ltok[:, g * 8:(g + 1) * 8], AF.Exp,
                                   bias=negr[:, g:g + 1], waits=[t_negr]))
            t_sum = p.dve("tensor_reduce", sume[:],
                          etok[:].rearrange("p (g e) -> p g e", e=8), AX.X,
                          ALU.add, waits=t_exp)
            t_rec = p.dve("reciprocal", mxp[:], sume[:], waits=[t_sum])
            t_mxp = p.dve("tensor_scalar_mul", mxp[:], mxp[:], half,
                          waits=[t_rec])
            t_oh = p.dve("tensor_tensor",
                         oh[:].rearrange("p (g e) -> p g e", e=8),
                         ltok[:].rearrange("p (g e) -> p g e", e=8),
                         rmax[:].rearrange("p (g o) -> p g o", o=1)
                         .broadcast_to([128, 4, 8]), ALU.is_ge,
                         waits=[t_rmax])
            t_ohs = p.dve("tensor_tensor",
                          ohs_tok[:].rearrange("p (g e) -> p g e", e=8),
                          oh[:].rearrange("p (g e) -> p g e", e=8),
                          mxp[:].rearrange("p (g o) -> p g o", o=1)
                          .broadcast_to([128, 4, 8]), ALU.mult,
                          waits=[t_oh, t_mxp])
            dbg_last = None
            if dbg:
                p.dma("sync", dbg_lem[0:8, cols], lem[:], waits=[t_lem])
                p.dma("sync", dbg_ltok[:, ti * 32:(ti + 1) * 32], ltok[:], waits=[t_ltok])
                p.dma("sync", dbg_rmax[:, ti * 4:(ti + 1) * 4], rmax[:], waits=[t_rmax])
                p.dma("sync", dbg_negr[:, ti * 4:(ti + 1) * 4], negr[:], waits=[t_negr])
                p.dma("sync", dbg_etok[:, ti * 32:(ti + 1) * 32], etok[:], waits=t_exp)
                p.dma("sync", dbg_sume[:, ti * 4:(ti + 1) * 4], sume[:], waits=[t_sum])
                p.dma("sync", dbg_mxp[:, ti * 4:(ti + 1) * 4], mxp[:], waits=[t_mxp])
                p.dma("sync", dbg_oh[:, ti * 32:(ti + 1) * 32], oh[:], waits=[t_oh])
                dbg_last = p.dma("sync", dbg_ohst[:, ti * 32:(ti + 1) * 32], ohs_tok[:], waits=[t_ohs])
            tt1 = None
            for g in range(4):
                tt1 = p.tr(pb[6][0:8, g * 128:(g + 1) * 128],
                           ohs_tok[:, g * 8:(g + 1) * 8], c.ident[:],
                           waits=[t_ohs])
            t_em = p.dve("tensor_copy", ohs_em[:], pb[6][0:8, 0:TT],
                         waits=[tt1, last_ohs_store])
            last_ohs_store = p.dma("sync", ohs_d[0:8, cols], ohs_em[:],
                                   waits=[t_em])
            route_done = t_em if not dbg else dbg_last
            prev = {"xn": ln["xn"], "xn_store": st, "route_done": route_done}

        def load_w1(e, war_tok, prev_tok):
            piece = []
            pr = prev_tok
            extra0 = [t_b2c, t_wrc, t_brc] if e == 0 else []
            for dc in range(DC):
                t_d = p.dma("scalar", wstage[:],
                            w1_d[e, dc * 128:(dc + 1) * 128, :],
                            waits=[pr, war_tok] + (extra0 if dc == 0 else []))
                pr = p.act("activation", w1_bf[:, dc * 2048:(dc + 1) * 2048],
                           wstage[:], AF.Copy, waits=[t_d, war_tok])
                piece.append(pr)
            return piece

        def load_w2(e, war_tok, prev_tok):
            piece = []
            pr = prev_tok
            for hc in range(16):
                t_d = p.dma("scalar", wstage[:, 0:1024],
                            w2_d[e, hc * 128:(hc + 1) * 128, :],
                            waits=[pr, war_tok])
                pr = p.dve("tensor_copy", w2_bf[:, hc * 1024:(hc + 1) * 1024],
                           wstage[:, 0:1024], waits=[t_d, war_tok])
                piece.append(pr)
            return piece

        w1_toks = load_w1(0, None, t_brc)
        w2_toks = load_w2(0, None, w1_toks[-1])
        xnb_war = [None, None]
        mask_war = None
        ph_war = [None, None]
        py_war = [None, None]
        yst_war = [None, None]
        hst_war = [None, None]
        last_y_dma = None
        h_end = None
        y_end = None
        prev_ydma = {}
        for e in range(8):
            for ti in range(NTt):
                cols = slice(ti * TT, (ti + 1) * TT)
                bi = ti % 2
                t_xn = p.dma("sync",
                             xnb[bi][:].rearrange("p (c t) -> p c t", t=TT),
                             xn_d[:, cols].rearrange("(c p) t -> p c t", p=128),
                             waits=[xnb_war[bi], last_xn_store])
                t_ohl = p.dma("sync", ohs_l[:], ohs_d[0:8, cols],
                              waits=[last_ohs_store, mask_war])
                t_mm_m = p.mm(pb[4][:, :], esel_sb[:, e * 128:(e + 1) * 128],
                              ohs_l[:], True, True,
                              waits=[t_ohl, mask_war, t_esel])
                t_mask = p.dve("tensor_copy", mask_bf[:], pb[4][:, :],
                               waits=[t_mm_m])
                mask_war = t_mask
                t_masks = []
                for hc in range(16):
                    ph = pb[hc % 2]
                    mmh = None
                    for dcc in range(DC):
                        w = []
                        if dcc == 0:
                            w = [ph_war[hc % 2], t_xn]
                        if ti == 0:
                            w = list(w) + [w1_toks[dcc]]
                        mmh = p.mm(ph[:, :],
                                   w1_bf[:, dcc * 2048 + hc * 128:
                                         dcc * 2048 + (hc + 1) * 128],
                                   xnb[bi][:, dcc * TT:(dcc + 1) * TT],
                                   dcc == 0, dcc == DC - 1, waits=w)
                    t_relu = p.act("activation", hstage[hc % 2][:], ph[:, :],
                                   AF.Relu,
                                   bias=b1c_sb[:, e * 16 + hc:e * 16 + hc + 1],
                                   waits=[mmh, t_b1c, hst_war[hc % 2]])
                    ph_war[hc % 2] = t_relu
                    t_m = p.dve("tensor_tensor",
                                h_sb[:, hc * TT:(hc + 1) * TT],
                                hstage[hc % 2][:], mask_bf[:], ALU.mult,
                                waits=[t_relu, t_mask])
                    hst_war[hc % 2] = t_m
                    t_masks.append(t_m)
                h_end = t_masks[-1]
                xnb_war[bi] = h_end
                for dcg in range(DC):
                    py = pb[2 + dcg % 2]
                    mmy = None
                    for hc in range(16):
                        w = []
                        if hc == 0:
                            w = [py_war[dcg % 2], t_masks[-1]]
                        if ti == 0:
                            w = list(w) + [w2_toks[hc]]
                        mmy = p.mm(py[:, :],
                                   w2_bf[:, hc * 1024 + dcg * 128:
                                         hc * 1024 + (dcg + 1) * 128],
                                   h_sb[:, hc * TT:(hc + 1) * TT],
                                   hc == 0, hc == 15, waits=w)
                    y_end = mmy
                    t_ev = p.act("activation", ystage[dcg % 2][:], py[:, :],
                                 AF.Copy, waits=[mmy, yst_war[dcg % 2]])
                    py_war[dcg % 2] = t_ev
                    kw = {} if e == 0 else {"accum_op": ALU.add}
                    t_yd = p.dma("gpsimd",
                                 y_d[dcg * 128:(dcg + 1) * 128, cols],
                                 ystage[dcg % 2][:],
                                 waits=[t_ev,
                                        ("sgp", p.cnt["sgp"], "gpsimd")], **kw)
                    prev_ydma[(dcg, ti)] = t_yd
                    yst_war[dcg % 2] = t_yd
                    last_y_dma = t_yd
            if e < 7:
                w1_toks = load_w1(e + 1, h_end, last_y_dma)
                w2_toks = load_w2(e + 1, y_end, w1_toks[-1])

        prev3 = {}
        store_toks = []
        for ti in range(NTt):
            cols = slice(ti * TT, (ti + 1) * TT)
            ldx, ldy = [], []
            for dc in range(DC):
                ldx.append(p.dma("sync", c.xt[:, dc * TT:(dc + 1) * TT],
                                 x_d[dc * 128:(dc + 1) * 128, cols],
                                 waits=[prev3.get("resid_done"), h_end]))
                ldy.append(p.dma("sync", ybuf[:, dc * TT:(dc + 1) * TT],
                                 y_d[dc * 128:(dc + 1) * 128, cols],
                                 waits=[last_y_dma, prev3.get("resid_done")]))
            t_o3 = p.dma("sync", ohs3[:], ohs_d[0:8, cols],
                         waits=[last_ohs_store, prev3.get("b2_done")])
            t_o3c = p.dve("tensor_copy", ohs3_bf[:], ohs3[:], waits=[t_o3])
            resid_done = None
            b2_done = None
            for dc in range(DC):
                t_b2mm = p.mm(pb[5][:, :],
                              b2_bf[0:8, dc * 128:(dc + 1) * 128], ohs3_bf[:],
                              True, True,
                              waits=[t_o3c, t_b2c, prev3.get("b2ev")])
                t_t2 = p.dve("scalar_tensor_tensor", c.t32[:], pb[5][:, :],
                             1.0, ybuf[:, dc * TT:(dc + 1) * TT], ALU.mult,
                             ALU.add, waits=[t_b2mm, ldy[dc]])
                prev3["b2ev"] = t_t2
                b2_done = t_t2
                ro = resout[dc % 2]
                t_ro = p.dve("tensor_tensor", ro[:], c.t32[:],
                             c.xt[:, dc * TT:(dc + 1) * TT], ALU.add,
                             waits=[ldx[dc], prev3.get(f"rost{dc % 2}")])
                t_st = p.dma("sync", out_d[dc * 128:(dc + 1) * 128, cols],
                             ro[:], waits=[t_ro])
                prev3[f"rost{dc % 2}"] = t_st
                resid_done = t_st
            prev3["resid_done"] = resid_done
            prev3["b2_done"] = b2_done
            store_toks.append(resid_done)

        p.barrier_end([store_toks[-1]])
        nc_final = p.finalize()
    return nc_final



def build_attn0(tokq=16384, parts="full"):
    NTt = tokq // TT
    nc = bass.Bass("TRN2", target_bir_lowering=False, debug=False)
    x_d = nc.dram_tensor("x", [D, tokq], F32, kind="ExternalInput").ap()
    wq_d = nc.dram_tensor("WqT", [D, 256], F32, kind="ExternalInput").ap()
    wk_d = nc.dram_tensor("WkT", [D, 256], F32, kind="ExternalInput").ap()
    wv_d = nc.dram_tensor("WvT", [D, 256], F32, kind="ExternalInput").ap()
    wg_d = nc.dram_tensor("WgT", [D, 256], F32, kind="ExternalInput").ap()
    bg_d = nc.dram_tensor("bg2", [128, 2], F32, kind="ExternalInput").ap()
    mask_d = nc.dram_tensor("mask4", [128, TT], F32, kind="ExternalInput").ap()
    ident_d = nc.dram_tensor("ident", [128, 128], F32, kind="ExternalInput").ap()
    o_d = nc.dram_tensor("o_heads", [tokq, 256], F32, kind="ExternalOutput").ap()

    p = P(nc)
    with contextlib.ExitStack() as es:
        c = alloc_common(nc, es, TT)
        sb = lambda name, shape, dt: es.enter_context(
            nc.sbuf_tensor(name, shape, dt))
        pb = [es.enter_context(nc.psum_tensor(f"pb{i}", [128, TT], F32))
              for i in range(8)]
        wq_bf = sb("wq_bf", [128, DC * 256], BF16)
        wk_bf = sb("wk_bf", [128, DC * 256], BF16)
        wv_bf = sb("wv_bf", [128, DC * 256], BF16)
        wg_bf = sb("wg_bf", [128, DC * 256], BF16)
        wstage = sb("wstage", [128, 1024], F32)
        bg_sb = sb("bg_sb", [128, 2], F32)
        mask_bf = sb("mask_bf", [128, TT], BF16)
        qf_bf = sb("qf_bf", [128, 2 * TT], BF16)
        qf_ev = sb("qf_ev", [128, 2 * TT], BF16)
        qf_od = sb("qf_od", [128, 2 * TT], BF16)
        kf_bf = sb("kf_bf", [128, 2 * TT], BF16)
        evcol = sb("evcol", [128, 1], F32)
        odcol = sb("odcol", [128, 1], F32)
        kft_bf = sb("kft_bf", [128, 4 * 256], BF16)
        vaug = sb("vaug", [128, 4 * 260], BF16)
        t1f = sb("t1f", [128, TT], F32)
        t2f = sb("t2f", [128, TT], F32)
        t3f = sb("t3f", [128, TT], F32)
        t4f = sb("t4f", [128, TT], F32)
        at_sb = sb("at_sb", [128, TT], BF16)
        o_sb = sb("o_sb", [128, 4 * 256], F32)
        rec4 = sb("rec4", [128, 8], F32)
        s_f = sb("s_f", [128, 2 * 65], F32)
        s_bf = sb("s_bf", [128, 2 * 65], BF16)

        ct = emit_const_init(p, c, ident_d)
        t_bg = p.dma("sync", bg_sb[:], bg_d)
        t_mk = p.dma("sync", wstage[:, 0:TT], mask_d)
        t_mkc = p.dve("tensor_copy", mask_bf[:], wstage[:, 0:TT], waits=[t_mk])
        wt = {}
        pr = t_mkc
        for nm, dram, buf in (("q", wq_d, wq_bf), ("k", wk_d, wk_bf),
                              ("v", wv_d, wv_bf), ("g", wg_d, wg_bf)):
            for dc in range(DC):
                t_d = p.dma("scalar", wstage[:, 0:256],
                            dram[dc * 128:(dc + 1) * 128, :], waits=[pr])
                pr = p.act("activation", buf[:, dc * 256:(dc + 1) * 256],
                           wstage[:, 0:256], AF.Copy, waits=[t_d])
            wt[nm] = pr
        t_sz = p.dve("memset", s_f[:], 0.0)
        s_tok = p.dve("memset", s_bf[:], 0.0, waits=[t_sz])
        p.dve("memset", evcol[0:64, :], 1.0)
        p.dve("memset", evcol[64:128, :], 0.0)
        p.dve("memset", odcol[0:64, :], 0.0)
        t_eo = p.dve("memset", odcol[64:128, :], 1.0)

        prev = {}
        last_store = None
        for ti in range(NTt):
            cols = slice(ti * TT, (ti + 1) * TT)
            ln = emit_ln(p, c, pb, x_d, cols,
                         xt_war=[prev.get("ln_done")],
                         xn_war=[prev.get("xn_users")],
                         pb_war=[prev.get("pb03")])
            qf_tok = kf_tok = None
            for fc in range(2):
                mmg = None
                for dc in range(DC):
                    w = []
                    if dc == 0:
                        w = [ln["xn"], wt["g"], prev.get("pg")]
                    mmg = p.mm(pb[0][:, 0:TT],
                               wg_bf[:, dc * 256 + fc * 128:
                                     dc * 256 + (fc + 1) * 128],
                               c.xn[:, dc * TT:(dc + 1) * TT],
                               dc == 0, dc == DC - 1, waits=w)
                t_sg = p.act("activation", t3f[:], pb[0][:, 0:TT], AF.Sigmoid,
                             bias=bg_sb[:, fc:fc + 1],
                             waits=[mmg, t_bg, prev.get("t3f")])
                prev["pg"] = t_sg
                t_sg1 = p.dve("tensor_scalar_add", t3f[:], t3f[:], 1e-6,
                              waits=[t_sg])
                mmq = None
                for dc in range(DC):
                    w = []
                    if dc == 0:
                        w = [ln["xn"], wt["q"], prev.get("pq")]
                    mmq = p.mm(pb[1][:, 0:TT],
                               wq_bf[:, dc * 256 + fc * 128:
                                     dc * 256 + (fc + 1) * 128],
                               c.xn[:, dc * TT:(dc + 1) * TT],
                               dc == 0, dc == DC - 1, waits=w)
                t_a = p.act("activation", t1f[:], pb[1][:, 0:TT], AF.Abs,
                            waits=[mmq, prev.get("t1f")])
                t_e = p.act("activation", t2f[:], t1f[:], AF.Exp,
                            scale=c.cc[:, CC_NONE:CC_NONE + 1],
                            waits=[t_a, prev.get("t2f"), c.cc_tok])
                t_l = p.act("activation", t1f[:], t2f[:], AF.Ln,
                            bias=c.cc[:, CC_ONE:CC_ONE + 1], waits=[t_e])
                t_r = p.dve("tensor_scalar_max", t2f[:], pb[1][:, 0:TT], 0.0,
                            waits=[t_e])
                prev["pq"] = t_r
                t_sp = p.dve("tensor_tensor", t4f[:], t1f[:], t2f[:], ALU.add,
                             waits=[t_l, t_r, prev.get("t4f")])
                prev["t1f"] = t_sp
                prev["t2f"] = t_sp
                qf_tok = p.dve("tensor_tensor", qf_bf[:, fc * TT:(fc + 1) * TT],
                               t4f[:], t3f[:], ALU.mult,
                               waits=[t_sp, t_sg1, prev.get("qf_use")])
                prev["t4f"] = qf_tok
                prev["t3f"] = qf_tok
                qf_tok = p.dve("tensor_scalar_mul",
                               qf_ev[:, fc * TT:(fc + 1) * TT],
                               qf_bf[:, fc * TT:(fc + 1) * TT], evcol[:],
                               waits=[qf_tok, t_eo, prev.get("qf_use")])
                qf_tok = p.dve("tensor_scalar_mul",
                               qf_od[:, fc * TT:(fc + 1) * TT],
                               qf_bf[:, fc * TT:(fc + 1) * TT], odcol[:],
                               waits=[qf_tok])
                mmk = None
                for dc in range(DC):
                    w = []
                    if dc == 0:
                        w = [ln["xn"], wt["k"], prev.get("pk")]
                    mmk = p.mm(pb[2][:, 0:TT],
                               wk_bf[:, dc * 256 + fc * 128:
                                     dc * 256 + (fc + 1) * 128],
                               c.xn[:, dc * TT:(dc + 1) * TT],
                               dc == 0, dc == DC - 1, waits=w)
                t_a = p.act("activation", t1f[:], pb[2][:, 0:TT], AF.Abs,
                            waits=[mmk, qf_tok])
                t_e = p.act("activation", t2f[:], t1f[:], AF.Exp,
                            scale=c.cc[:, CC_NONE:CC_NONE + 1], waits=[t_a])
                t_l = p.act("activation", t1f[:], t2f[:], AF.Ln,
                            bias=c.cc[:, CC_ONE:CC_ONE + 1], waits=[t_e])
                t_r = p.dve("tensor_scalar_max", t2f[:], pb[2][:, 0:TT], 0.0,
                            waits=[t_e])
                prev["pk"] = t_r
                kf_tok = p.dve("tensor_tensor", kf_bf[:, fc * TT:(fc + 1) * TT],
                               t1f[:], t2f[:], ALU.add,
                               waits=[t_l, t_r, prev.get("kf_use")])
                prev["t1f"] = kf_tok
                prev["t2f"] = kf_tok
            t_vz = p.dve("memset", vaug[:], 1.0, waits=[prev.get("vaug_use")])
            vk_toks = []
            for tg in range(4):
                pv = pb[4 + tg % 2]
                pkt = pb[6 + tg % 2]
                mmv = None
                for dc in range(DC):
                    w = []
                    if dc == 0:
                        w = [ln["xn"], wt["v"], prev.get(f"pv{tg % 2}")]
                    mmv = p.mm(pv[:, 0:256],
                               c.xn[:, dc * TT + tg * 128:
                                    dc * TT + (tg + 1) * 128],
                               wv_bf[:, dc * 256:(dc + 1) * 256],
                               dc == 0, dc == DC - 1, waits=w)
                mmkt = None
                for dc in range(DC):
                    w = []
                    if dc == 0:
                        w = [ln["xn"], wt["k"], prev.get(f"pkt{tg % 2}")]
                    mmkt = p.mm(pkt[:, 0:256],
                                c.xn[:, dc * TT + tg * 128:
                                     dc * TT + (tg + 1) * 128],
                                wk_bf[:, dc * 256:(dc + 1) * 256],
                                dc == 0, dc == DC - 1, waits=w)
                t_v = p.dve("tensor_copy",
                            vaug[:, tg * 260:(tg + 1) * 260]
                            .rearrange("p (h e) -> p h e", e=65)[:, :, 0:64],
                            pv[:, 0:256].rearrange("p (h e) -> p h e", e=64),
                            waits=[mmv, t_vz])
                prev[f"pv{tg % 2}"] = t_v
                t_a = p.act("activation", t1f[:, 0:256], pkt[:, 0:256],
                            AF.Abs,
                            waits=[mmkt,
                                   kf_tok if tg == 0 else vk_toks[tg - 1][1]])
                t_e = p.act("activation", t2f[:, 0:256], t1f[:, 0:256],
                            AF.Exp, scale=c.cc[:, CC_NONE:CC_NONE + 1],
                            waits=[t_a])
                t_l = p.act("activation", t1f[:, 0:256], t2f[:, 0:256],
                            AF.Ln, bias=c.cc[:, CC_ONE:CC_ONE + 1],
                            waits=[t_e])
                t_r = p.dve("tensor_scalar_max", t2f[:, 0:256],
                            pkt[:, 0:256], 0.0, waits=[t_e])
                prev[f"pkt{tg % 2}"] = t_r
                t_kt = p.dve("tensor_tensor",
                             kft_bf[:, tg * 256:(tg + 1) * 256],
                             t1f[:, 0:256], t2f[:, 0:256], ALU.add,
                             waits=[t_l, t_r])
                vk_toks.append((t_v, t_kt))
            prev["t1f"] = vk_toks[-1][1]
            prev["t2f"] = vk_toks[-1][1]
            prev["ln_done"] = vk_toks[-1][1]
            prev["xn_users"] = vk_toks[-1][1]
            prev["pb03"] = vk_toks[-1][1]
            if parts == "proj":
                t_oz = p.dve("memset", o_sb[:], 0.125,
                             waits=[prev.get("osb_use"), vk_toks[-1][1]])
                st = None
                for tg in range(4):
                    st = p.dma("sync", o_d[ti * TT + tg * 128:
                                           ti * TT + (tg + 1) * 128, :],
                               o_sb[:, tg * 256:(tg + 1) * 256],
                               waits=[t_oz])
                prev["osb_use"] = st
                last_store = st
                prev["po_use"] = t_oz
                prev["vaug_use"] = t_oz
                prev["at_use"] = t_oz
                continue
            for tg in range(4):
                t_v, t_kt = vk_toks[tg]
                mmat = None
                for h in (() if parts in ("noat",) else range(4)):
                    fc = h // 2
                    qv = qf_ev if h % 2 == 0 else qf_od
                    w = [kf_tok, qf_tok] if h == 0 else []
                    if h == 0 and "at_use" in prev:
                        w = list(w) + [prev["at_use"]]
                    mmat = p.mm(pb[3][:, h * 128:(h + 1) * 128],
                                kf_bf[:, fc * TT + tg * 128:
                                      fc * TT + (tg + 1) * 128],
                                qv[:, fc * TT + tg * 128:
                                   fc * TT + (tg + 1) * 128],
                                True, True, waits=w)
                if parts == "noat":
                    t_at = p.dve("memset", at_sb[:], 0.5,
                                 waits=[prev.get("at_use")])
                elif parts == "noo2":
                    t_at = p.dve("tensor_copy", at_sb[:], pb[3][:, :],
                                 waits=[mmat])
                else:
                    t_at = p.dve("tensor_tensor", at_sb[:], pb[3][:, :],
                                 mask_bf[:], ALU.mult, waits=[mmat])
                prev["at_use"] = t_at
                mmo2 = None
                for h in (() if parts in ("noo", "noo2") else range(4)):
                    fc = h // 2
                    qv = qf_ev if h % 2 == 0 else qf_od
                    w = [t_at, t_v] if h == 0 else []
                    if h == 0 and "po_use" in prev:
                        w = list(w) + [prev["po_use"]]
                    p.mm(pb[4][:, h * 65:(h + 1) * 65],
                         at_sb[:, h * 128:(h + 1) * 128],
                         vaug[:, tg * 260 + h * 65:tg * 260 + (h + 1) * 65],
                         True, False, waits=w)
                    scols = slice((h // 2) * 65, (h // 2) * 65 + 65)
                    mmo2 = p.mm(pb[4][:, h * 65:(h + 1) * 65],
                                qv[:, fc * TT + tg * 128:
                                   fc * TT + (tg + 1) * 128],
                                s_bf[:, scols],
                                False, True, waits=[s_tok])
                mmds = None
                for h in range(4):
                    if parts == "nods":
                        break
                    w = [t_kt] if h == 0 else []
                    if h == 0 and "pds_use" in prev:
                        w = list(w) + [prev["pds_use"]]
                    mmds = p.mm(pb[5][(h % 2) * 64:(h % 2) * 64 + 64,
                                      (h // 2) * 65:(h // 2) * 65 + 65],
                                kft_bf[:, tg * 256 + h * 64:
                                       tg * 256 + (h + 1) * 64],
                                vaug[:, tg * 260 + h * 65:tg * 260 + (h + 1) * 65],
                                True, True, waits=w)
                if parts in ("nos", "nods"):
                    pass
                else:
                    t_sa = p.dve("tensor_tensor", s_f[:], s_f[:],
                                 pb[5][:, 0:130], ALU.add, waits=[mmds])
                    prev["pds_use"] = t_sa
                    s_tok = p.dve("tensor_copy", s_bf[:], s_f[:],
                                  waits=[t_sa, mmo2])
                if parts in ("noo", "noo2"):
                    t_o = p.dve("memset", o_sb[:, tg * 256:(tg + 1) * 256],
                                0.125, waits=[t_at, prev.get("osb_use")])
                elif parts == "nod":
                    t_o = p.dve("tensor_copy", o_sb[:, tg * 256:(tg + 1) * 256],
                                pb[4][:, 0:256],
                                waits=[mmo2, prev.get("osb_use")])
                else:
                    t_d1 = p.dve("tensor_scalar_add", rec4[:, 0:4],
                                 pb[4][:, 0:260].rearrange(
                                     "p (h e) -> p h e", e=65)[:, :, 64:65],
                                 1e-6, waits=[mmo2])
                    t_d2 = p.dve("reciprocal", rec4[:, 4:8], rec4[:, 0:4],
                                 waits=[t_d1])
                    t_o = p.dve("tensor_tensor",
                                o_sb[:, tg * 256:(tg + 1) * 256]
                                .rearrange("p (h e) -> p h e", e=64),
                                pb[4][:, 0:260].rearrange(
                                    "p (h e) -> p h e", e=65)[:, :, 0:64],
                                rec4[:, 4:8].rearrange(
                                    "p (h o) -> p h o", o=1)
                                .broadcast_to([128, 4, 64]), ALU.mult,
                                waits=[t_d2, prev.get("osb_use")])
                prev["po_use"] = t_o
                prev["vaug_use"] = t_o
            st = None
            for tg in range(4):
                st = p.dma("sync", o_d[ti * TT + tg * 128:
                                       ti * TT + (tg + 1) * 128, :],
                           o_sb[:, tg * 256:(tg + 1) * 256],
                           waits=[prev["po_use"]])
            prev["osb_use"] = st
            last_store = st

        p.barrier_end([last_store])
        nc_final = p.finalize()
    return nc_final


def prep_attn0_inputs(Wq, Wk, Wv, Wo_unused, Wg, bg, heads):
    h0 = heads[0]
    colsl = slice(h0 * 64, (h0 + 4) * 64)
    mask4 = np.tril(np.ones((128, 128), np.float32))
    mask4 = np.concatenate([mask4.T] * 4, axis=1)
    return dict(WqT=_f32(Wq.T[:, colsl]), WkT=_f32(Wk.T[:, colsl]),
                WvT=_f32(Wv.T[:, colsl]), WgT=_f32(Wg.T[:, colsl]),
                bg2=_f32(np.asarray(bg)[colsl].reshape(2, 128).T),
                mask4=mask4, ident=np.eye(128, dtype=np.float32))


def run_attn0(resids_full, weights, tokq=16384):
    nc = build_attn0(tokq)
    in_maps = []
    for cc_ in range(NCORES):
        b, hq = cc_ // 4, (cc_ % 4) * 4
        wts = prep_attn0_inputs(weights["Wq"], weights["Wk"], weights["Wv"],
                                None, weights["Wg"], weights["bg"],
                                [hq])
        in_maps.append(dict(x=resids_full[b], **wts))
    outs = _run(nc, in_maps, ["o_heads"])
    return [o["o_heads"] for o in outs]


CK = 64
NCHT = TT // CK


def build_attn1(tokq=16384):
    NTt = tokq // TT
    nc = bass.Bass("TRN2", target_bir_lowering=False, debug=False)
    x_d = nc.dram_tensor("x", [D, tokq], F32, kind="ExternalInput").ap()
    wcd_d = nc.dram_tensor("wcdiag", [3 * DC * 4 * 128, 128], F32,
                           kind="ExternalInput").ap()
    bcv_d = nc.dram_tensor("bconv", [128, 3 * DC], F32, kind="ExternalInput").ap()
    wq_d = nc.dram_tensor("WqT", [D, 256], F32, kind="ExternalInput").ap()
    wk_d = nc.dram_tensor("WkT", [D, 256], F32, kind="ExternalInput").ap()
    wv_d = nc.dram_tensor("WvT", [D, 256], F32, kind="ExternalInput").ap()
    bq_d = nc.dram_tensor("bq2", [128, 2], F32, kind="ExternalInput").ap()
    bk_d = nc.dram_tensor("bk2", [128, 2], F32, kind="ExternalInput").ap()
    bv_d = nc.dram_tensor("bvrow", [1, 256], F32, kind="ExternalInput").ap()
    wad_d = nc.dram_tensor("WadT", [D, 64], F32, kind="ExternalInput").ap()
    bad_d = nc.dram_tensor("bad", [64, 1], F32, kind="ExternalInput").ap()
    wau_d = nc.dram_tensor("WauT", [64, 256], F32, kind="ExternalInput").ap()
    bau_d = nc.dram_tensor("bau2", [128, 2], F32, kind="ExternalInput").ap()
    wbe_d = nc.dram_tensor("WbeT", [D, 4], F32, kind="ExternalInput").ap()
    bbe_d = nc.dram_tensor("bbe", [4, 1], F32, kind="ExternalInput").ap()
    wgd_d = nc.dram_tensor("WgdT", [D, 64], F32, kind="ExternalInput").ap()
    bgd_d = nc.dram_tensor("bgd", [64, 1], F32, kind="ExternalInput").ap()
    wgu_d = nc.dram_tensor("WguT", [64, 4], F32, kind="ExternalInput").ap()
    bgu_d = nc.dram_tensor("bgu", [4, 1], F32, kind="ExternalInput").ap()
    rms_d = nc.dram_tensor("rmsfull", [128, 256], F32, kind="ExternalInput").ap()
    amask_d = nc.dram_tensor("amask", [128, 512], F32, kind="ExternalInput").ap()
    lmask_d = nc.dram_tensor("lmask", [128, 512], F32, kind="ExternalInput").ap()
    umask_d = nc.dram_tensor("umask", [128, 512], F32, kind="ExternalInput").ap()
    e2c_d = nc.dram_tensor("e2c", [128, 2], F32, kind="ExternalInput").ap()
    e2t_d = nc.dram_tensor("e2t", [2, 128], F32, kind="ExternalInput").ap()
    f2c_d = nc.dram_tensor("f2c", [4, 2 * 128], F32, kind="ExternalInput").ap()
    ident_d = nc.dram_tensor("ident", [128, 128], F32, kind="ExternalInput").ap()
    o_d = nc.dram_tensor("o_heads", [tokq, 256], F32, kind="ExternalOutput").ap()

    HT = TT + 3
    p = P(nc)
    with contextlib.ExitStack() as es:
        c = alloc_common(nc, es, TT)
        sb = lambda name, shape, dt: es.enter_context(
            nc.sbuf_tensor(name, shape, dt))
        pb = [es.enter_context(nc.psum_tensor(f"pb{i}", [128, TT], F32))
              for i in range(6)]
        pbt = es.enter_context(nc.psum_tensor("pbt", [128, 256], BF16))
        pbO = es.enter_context(nc.psum_tensor("pbO", [128, 512], F32))
        xnh = sb("xnh", [128, DC * HT], BF16)
        wdg_bf = sb("wdg_bf", [128, 3 * DC * 4 * 128], BF16)
        bcv_sb = sb("bcv_sb", [128, 3 * DC], F32)
        xcq = sb("xcq", [128, DC * TT], BF16)
        xck = sb("xck", [128, DC * TT], BF16)
        xcv = sb("xcv", [128, DC * TT], BF16)
        wq_bf = sb("wq_bf", [128, DC * 256], BF16)
        wk_bf = sb("wk_bf", [128, DC * 256], BF16)
        wv_bf = sb("wv_bf", [128, DC * 256], BF16)
        wad_bf = sb("wad_bf", [128, DC * 64], BF16)
        wau_bf = sb("wau_bf", [64, 256], BF16)
        wbe_bf = sb("wbe_bf", [128, DC * 4], BF16)
        wgd_bf = sb("wgd_bf", [128, DC * 64], BF16)
        wgu_bf = sb("wgu_bf", [64, 4], BF16)
        bvec = sb("bvec", [128, 16], F32)
        rms_sb = sb("rms_sb", [128, 256], F32)
        amask = sb("amask_s", [128, 512], BF16)
        lmask = sb("lmask_s", [128, 512], BF16)
        umask = sb("umask_s", [128, 512], BF16)
        e2c_sb = sb("e2c_sb", [128, 2], F32)
        e2t_sb = sb("e2t_sb", [2, 128], F32)
        f2c_sb = sb("f2c_sb", [4, 2 * 128], F32)
        ident_sb = c.ident
        wstage = sb("wstage", [128, 2048], F32)
        qn_bf = sb("qn_bf", [128, 2 * TT], BF16)
        qn_ev = sb("qn_ev", [128, 2 * TT], BF16)
        qn_od = sb("qn_od", [128, 2 * TT], BF16)
        kn_bf = sb("kn_bf", [128, 2 * TT], BF16)
        v_tile = sb("v_tile", [128, NCHT * 256], BF16)
        la_sb = sb("la_sb", [128, 2 * TT], F32)
        cl_sb = sb("cl_sb", [128, 2 * TT], F32)
        clcar = sb("clcar", [128, 2], F32)
        zz = sb("zz", [128, TT], F32)
        beta_sb = sb("beta_sb", [4, TT], F32)
        g_sb = sb("g_sb", [4, TT], F32)
        t1f = sb("a_t1f", [128, TT], F32)
        t2f = sb("a_t2f", [128, TT], F32)
        lr_bf = sb("lr_bf", [64, TT], BF16)
        identb = sb("identb", [128, 128], BF16)
        bvrow_bf = sb("bvrow_bf", [1, 256], BF16)
        l2_sb = sb("a_l2", [128, 512], BF16)
        rn_sb = sb("rn_sb", [2, 2 * TT], F32)
        qpre = sb("qpre", [128, TT], F32)
        evcol = sb("a_evcol", [128, 1], F32)
        odcol = sb("a_odcol", [128, 1], F32)
        qt_pad = sb("qt_pad", [128, 2 * 128], BF16)
        kh_pad = sb("kh_pad", [128, 2 * 128], BF16)
        kt_pad = sb("kt_pad", [128, 2 * 128], BF16)
        kt_ev = sb("kt_ev", [128, 2 * 128], BF16)
        kt_od = sb("kt_od", [128, 2 * 128], BF16)
        kc_pad = sb("kc_pad", [128, 2 * 128], BF16)
        kc_ev = sb("kc_ev", [128, 2 * 128], BF16)
        kc_od = sb("kc_od", [128, 2 * 128], BF16)
        k2_tok = sb("k2_tok", [128, 2 * 128], BF16)
        k2_fm = sb("k2_fm", [128, 2 * 128], BF16)
        at_sb = sb("a_at", [128, 512], BF16)
        l_sb = sb("a_l", [128, 512], BF16)
        u_sb = sb("a_u", [128, 512], BF16)
        upow = [sb(f"a_up{i}", [128, 512], BF16) for i in range(6)]
        w_sb = sb("a_w", [128, 256], BF16)
        tw = [sb(f"a_tw{i}", [128, 256], BF16) for i in range(2)]
        oin = sb("a_oin", [64, 256], F32)
        osum = sb("a_osum", [64, 256], F32)
        osc = sb("a_osc", [64, 256], F32)
        ssq4 = sb("a_ssq", [64, 12], F32)
        gt_c = sb("a_gt", [64, 4], F32)
        cols1 = sb("a_cols", [128, 8], F32)
        s_f = sb("a_sf", [128, 2 * 64], F32)
        s_bf = sb("a_sbf", [128, 2 * 64], BF16)

        ct = emit_const_init(p, c, ident_d)
        ld0 = [p.dma("sync", bcv_sb[:], bcv_d), p.dma("sync", rms_sb[:], rms_d),
               p.dma("sync", e2c_sb[:], e2c_d), p.dma("sync", e2t_sb[:], e2t_d)]
        t_idb = p.dve("tensor_copy", identb[:], c.ident[:], waits=[ct[-2], ct[-1]])
        t_am = p.dma("sync", wstage[:, 0:512], amask_d)
        t_amc = p.dve("tensor_copy", amask[:], wstage[:, 0:512], waits=[t_am])
        t_lm = p.dma("sync", wstage[:, 512:1024], lmask_d, waits=[t_amc])
        t_lmc = p.dve("tensor_copy", lmask[:], wstage[:, 512:1024], waits=[t_lm])
        t_um = p.dma("sync", wstage[:, 1024:1536], umask_d, waits=[t_lmc])
        t_umc = p.dve("tensor_copy", umask[:], wstage[:, 1024:1536], waits=[t_um])
        t_f2c = p.dma("sync", f2c_sb[:], f2c_d)
        t_bq = p.dma("sync", bvec[:, 0:2], bq_d)
        t_bk = p.dma("sync", bvec[:, 2:4], bk_d)
        t_bv = p.dma("sync", wstage[0:1, 1792:2048], bv_d)
        t_bvr = p.dve("tensor_copy", bvrow_bf[:], wstage[0:1, 1792:2048],
                      waits=[t_bv, t_f2c])
        t_bau = p.dma("sync", bvec[:, 6:8], bau_d)
        t_bad = p.dma("sync", bvec[0:64, 8:9], bad_d)
        t_bbe = p.dma("sync", bvec[0:4, 9:10], bbe_d)
        t_bgd = p.dma("sync", bvec[0:64, 10:11], bgd_d)
        t_bgu = p.dma("sync", bvec[0:4, 11:12], bgu_d)
        t_bias = t_bgu
        pr = t_f2c
        wt = {}
        for nm, dram, buf, ncols in (("wcd", wcd_d, wdg_bf, None),
                                     ("q", wq_d, wq_bf, 256),
                                     ("k", wk_d, wk_bf, 256),
                                     ("v", wv_d, wv_bf, 256),
                                     ("ad", wad_d, wad_bf, 64),
                                     ("be", wbe_d, wbe_bf, 4),
                                     ("gd", wgd_d, wgd_bf, 64)):
            if nm == "wcd":
                for i in range(3 * DC * 4 * 128 // 128):
                    t_dd = p.dma("scalar", wstage[:, 0:128],
                                 wcd_d[i * 128:(i + 1) * 128, :], waits=[pr])
                    pr = p.act("activation", buf[:, i * 128:(i + 1) * 128],
                               wstage[:, 0:128], AF.Copy, waits=[t_dd])
            else:
                for dc in range(DC):
                    t_dd = p.dma("scalar", wstage[:, 0:ncols],
                                 dram[dc * 128:(dc + 1) * 128, :], waits=[pr])
                    pr = p.act("activation", buf[:, dc * ncols:(dc + 1) * ncols],
                               wstage[:, 0:ncols], AF.Copy, waits=[t_dd])
            wt[nm] = pr
        t_dd = p.dma("scalar", wstage[0:64, 0:256], wau_d, waits=[pr])
        pr = p.act("activation", wau_bf[:], wstage[0:64, 0:256], AF.Copy,
                   waits=[t_dd])
        wt["au"] = pr
        t_dd = p.dma("scalar", wstage[0:64, 256:260], wgu_d, waits=[pr])
        pr = p.act("activation", wgu_bf[:], wstage[0:64, 256:260], AF.Copy,
                   waits=[t_dd])
        wt["gu"] = pr
        z0 = [p.dve("memset", s_f[:], 0.0), p.dve("memset", s_bf[:], 0.0),
              p.dve("memset", zz[:], 0.0), p.dve("memset", clcar[:], 0.0),
              p.dve("memset", qt_pad[:], 0.0), p.dve("memset", kh_pad[:], 0.0),
              p.dve("memset", kt_pad[:], 0.0), p.dve("memset", kt_ev[:], 0.0),
              p.dve("memset", kt_od[:], 0.0), p.dve("memset", kc_pad[:], 0.0),
              p.dve("memset", kc_ev[:], 0.0), p.dve("memset", kc_od[:], 0.0),
              p.dve("memset", k2_fm[:], 0.0), p.dve("memset", v_tile[:], 0.0),
              p.dve("memset", evcol[0:64, :], 1.0),
              p.dve("memset", evcol[64:128, :], 0.0),
              p.dve("memset", odcol[0:64, :], 0.0),
              p.dve("memset", odcol[64:128, :], 1.0)]
        for dc in range(DC):
            z0.append(p.dve("memset", xnh[:, dc * HT:dc * HT + 3], 0.0))
        s_tok = z0[1]
        z_tok = z0[-1]

        prev = {}
        last_store = None
        for ti in range(NTt):
            cols = slice(ti * TT, (ti + 1) * TT)
            hl = None
            if ti > 0:
                for dc in range(DC):
                    hl = p.dve("tensor_copy", xnh[:, dc * HT:dc * HT + 3],
                               xnh[:, dc * HT + TT:dc * HT + TT + 3],
                               waits=[prev.get("conv_done")])
            ln = emit_ln(p, c, pb, x_d, cols,
                         xt_war=[prev.get("ln_done")],
                         xn_war=[prev.get("conv_done"), hl],
                         pb_war=[prev.get("pb03")],
                         xn_ap_fn=lambda dc: xnh[:, dc * HT + 3:dc * HT + 3 + TT])
            conv_done = None
            for icv, (xcb, bco) in enumerate(((xcq, 0), (xck, 1), (xcv, 2))):
                for dc in range(DC):
                    mmc = None
                    for tau in range(4):
                        w = []
                        if tau == 0:
                            w = [ln["xn"], wt["wcd"], prev.get(f"pcv{dc % 2}")]
                        mmc = p.mm(pb[dc % 2][:, 0:TT],
                                   wdg_bf[:, ((icv * DC + dc) * 4 + tau) * 128:
                                          ((icv * DC + dc) * 4 + tau + 1) * 128],
                                   xnh[:, dc * HT + tau:dc * HT + tau + TT],
                                   tau == 0, tau == 3, waits=w)
                    conv_done = p.act("activation", xcb[:, dc * TT:(dc + 1) * TT],
                                      pb[dc % 2][:, 0:TT], AF.Silu,
                                      bias=bcv_sb[:, icv * DC + dc:
                                                  icv * DC + dc + 1],
                                      waits=[mmc, ld0[0]])
                    prev[f"pcv{dc % 2}"] = conv_done
            prev["conv_done"] = conv_done
            for iqk, (wbuf, xcb, bcol, outb) in enumerate(
                    ((wq_bf, xcq, 0, qn_bf), (wk_bf, xck, 2, kn_bf))):
                for fc in range(2):
                    mmp = None
                    for dc in range(DC):
                        w = []
                        if dc == 0:
                            w = [prev["conv_done"], wt["q" if iqk == 0 else "k"],
                                 prev.get("pqk")]
                        mmp = p.mm(pb[2][:, 0:TT],
                                   wbuf[:, dc * 256 + fc * 128:
                                        dc * 256 + (fc + 1) * 128],
                                   xcb[:, dc * TT:(dc + 1) * TT],
                                   dc == 0, dc == DC - 1, waits=w)
                    t_qp = p.dve("tensor_scalar_add", qpre[:], pb[2][:, 0:TT],
                                 bvec[:, bcol + fc:bcol + fc + 1],
                                 waits=[mmp, t_bias, prev.get("qpre")])
                    prev["pqk"] = t_qp
                    t_sq = p.act("activation", t1f[:], qpre[:], AF.Square,
                                 waits=[t_qp, prev.get("t1f")])
                    mmsq = p.mm(pb[3][0:2, 0:TT],
                                e2c_sb[:, 0:2], t1f[:], True, True,
                                waits=[t_sq, ld0[2], prev.get("pssq")])
                    t_mx = p.dve("tensor_scalar_max",
                                 rn_sb[0:2, fc * TT:(fc + 1) * TT],
                                 pb[3][0:2, 0:TT], 1e-6, waits=[mmsq])
                    t_sqr = p.act("activation", rn_sb[0:2, fc * TT:(fc + 1) * TT],
                                  rn_sb[0:2, fc * TT:(fc + 1) * TT], AF.Sqrt,
                                  waits=[t_mx])
                    t_rc = p.dve("reciprocal", rn_sb[0:2, fc * TT:(fc + 1) * TT],
                                 rn_sb[0:2, fc * TT:(fc + 1) * TT],
                                 waits=[t_sqr])
                    prev["pssq"] = t_rc
                    mmrn = p.mm(pb[4][:, 0:TT], e2t_sb[:, :],
                                rn_sb[0:2, fc * TT:(fc + 1) * TT], True, True,
                                waits=[t_rc, ld0[3], prev.get("prn")])
                    t_n = p.dve("tensor_tensor", outb[:, fc * TT:(fc + 1) * TT],
                                qpre[:], pb[4][:, 0:TT], ALU.mult,
                                waits=[mmrn, t_qp, prev.get("qn_use")])
                    prev["prn"] = t_n
                    prev["qpre"] = t_n
                    prev["t1f"] = t_sq
            t_ev = p.dve("tensor_scalar_mul", qn_ev[:], qn_bf[:], evcol[:],
                         waits=[prev["prn"], z_tok, prev.get("qn_use")])
            t_od = p.dve("tensor_scalar_mul", qn_od[:], qn_bf[:], odcol[:],
                         waits=[t_ev])
            qn_tok = t_od
            v_toks = []
            for cc_ in range(NCHT):
                for dc in range(DC):
                    w = []
                    if dc == 0:
                        w = [prev["conv_done"], wt["v"], prev.get("pv"),
                             prev.get("po_use")]
                    p.mm(pbO[0:64, 0:256],
                         xcv[:, dc * TT + cc_ * CK:dc * TT + (cc_ + 1) * CK],
                         wv_bf[:, dc * 256:(dc + 1) * 256],
                         dc == 0, False, waits=w)
                mmv = p.mm(pbO[0:64, 0:256], c.onesrow_bf[0:1, 0:64],
                           bvrow_bf[:], False, True, waits=[t_bvr])
                t_vv = p.dve("tensor_copy",
                             v_tile[0:64, cc_ * 256:(cc_ + 1) * 256],
                             pbO[0:64, 0:256],
                             waits=[mmv, prev.get("v_use")])
                prev["pv"] = t_vv
                v_toks.append(t_vv)
            mma = None
            for dc in range(DC):
                w = ([ln["xn"], wt["ad"], prev.get("plr"),
                      prev.get("pb5_war")] if dc == 0 else [])
                mma = p.mm(pb[5][0:64, 0:TT], wad_bf[:, dc * 64:(dc + 1) * 64],
                           xnh[:, dc * HT + 3:dc * HT + 3 + TT],
                           dc == 0, dc == DC - 1, waits=w)
            t_t1 = p.act("activation", t1f[0:64, :], pb[5][0:64, 0:TT], AF.Silu,
                         bias=bvec[0:64, 8:9], waits=[mma, t_bias,
                                                      prev.get("t1f")])
            t_t1c = p.dve("tensor_copy", lr_bf[0:64, 0:TT], t1f[0:64, :],
                          waits=[t_t1])
            for fc in range(2):
                mmau = p.mm(pb[5][:, 0:TT], wau_bf[:, fc * 128:(fc + 1) * 128],
                            lr_bf[0:64, 0:TT], True, True,
                            waits=[t_t1c, wt["au"], prev.get("plr")])
                t_as = p.act("activation", t2f[:], pb[5][:, 0:TT], AF.Sigmoid,
                             bias=bvec[:, 6 + fc:7 + fc],
                             waits=[mmau, prev.get("t2f")])
                t_mx = p.dve("tensor_scalar_max", t2f[:], t2f[:], 0.1,
                             waits=[t_as])
                t_ln = p.act("activation", la_sb[:, fc * TT:(fc + 1) * TT],
                             t2f[:], AF.Ln, waits=[t_mx, prev.get("la_use")])
                prev["t2f"] = t_ln
                prev["plr"] = t_ln
            mmb = None
            for dc in range(DC):
                w = ([ln["xn"], wt["be"], prev.get("pbe"), prev["plr"]]
                     if dc == 0 else [])
                mmb = p.mm(pb[5][0:4, 0:TT], wbe_bf[:, dc * 4:(dc + 1) * 4],
                           xnh[:, dc * HT + 3:dc * HT + 3 + TT],
                           dc == 0, dc == DC - 1, waits=w)
            t_be = p.act("activation", beta_sb[:], pb[5][0:4, 0:TT], AF.Sigmoid,
                         bias=bvec[0:4, 9:10], waits=[mmb, prev.get("beta_use")])
            prev["pbe"] = t_be
            mmg = None
            for dc in range(DC):
                w = [ln["xn"], wt["gd"], t_be] if dc == 0 else []
                mmg = p.mm(pb[5][0:64, 0:TT], wgd_bf[:, dc * 64:(dc + 1) * 64],
                           xnh[:, dc * HT + 3:dc * HT + 3 + TT],
                           dc == 0, dc == DC - 1, waits=w)
            t_t2 = p.act("activation", t1f[0:64, :], pb[5][0:64, 0:TT], AF.Silu,
                         bias=bvec[0:64, 10:11], waits=[mmg, t_t1c])
            t_t2c = p.dve("tensor_copy", lr_bf[0:64, 0:TT], t1f[0:64, :],
                          waits=[t_t2])
            mmgu = p.mm(pb[5][0:4, 0:TT], wgu_bf[:, 0:4], lr_bf[0:64, 0:TT],
                        True, True, waits=[t_t2c, wt["gu"]])
            t_g = p.act("activation", g_sb[:], pb[5][0:4, 0:TT], AF.Sigmoid,
                        bias=bvec[0:4, 11:12], waits=[mmgu,
                                                      prev.get("g_use")])
            prev["pbe"] = t_g
            prev["pbf"] = t_g
            cl_toks = []
            for fc in range(2):
                t_cc = p.dve("tensor_copy", clcar[:, fc:fc + 1],
                             cl_sb[:, fc * TT + TT - 1:fc * TT + TT],
                             waits=[prev.get("cl_use")]) if ti > 0 else None
                t_cl = p.dve("tensor_tensor_scan",
                             cl_sb[:, fc * TT:(fc + 1) * TT],
                             la_sb[:, fc * TT:(fc + 1) * TT], zz[:],
                             clcar[:, fc:fc + 1], ALU.add, ALU.bypass,
                             waits=[prev["plr"], t_cc, z0[3], z0[2]])
                cl_toks.append(t_cl)
            prev["cl_use"] = cl_toks[-1]
            prev["ln_done"] = cl_toks[-1]
            prev["pb03"] = cl_toks[-1]
            prev["la_use"] = cl_toks[-1]

            for cc_ in range(NCHT):
                ch = slice(cc_ * CK, (cc_ + 1) * CK)
                colt = []
                for fc in range(2):
                    base_col = (fc * TT + cc_ * CK)
                    t_nm = p.dve("tensor_scalar_mul", cols1[:, fc:fc + 1],
                                 cl_sb[:, base_col + 31:base_col + 32], -1.0,
                                 waits=[cl_toks[fc], prev.get("cols_use")])
                    if ti == 0 and cc_ == 0:
                        t_nb = p.dve("memset", cols1[:, 2 + fc:3 + fc], 0.0,
                                     waits=[t_nm])
                    else:
                        pcol = (fc * TT + cc_ * CK - 1) if cc_ > 0 else None
                        src_ap = (cl_sb[:, pcol:pcol + 1] if cc_ > 0 else
                                  clcar[:, fc:fc + 1])
                        t_nb = p.dve("tensor_scalar_mul", cols1[:, 2 + fc:3 + fc],
                                     src_ap, -1.0, waits=[t_nm])
                    t_e2 = p.act("activation", cols1[:, 4 + fc:5 + fc],
                                 cl_sb[:, base_col + 63:base_col + 64], AF.Exp,
                                 bias=cols1[:, fc:fc + 1],
                                 waits=[t_nb, prev.get("acts")])
                    t_gc = p.act("activation", cols1[:, 6 + fc:7 + fc],
                                 cl_sb[:, base_col + 63:base_col + 64], AF.Exp,
                                 bias=cols1[:, 2 + fc:3 + fc], waits=[t_e2])
                    colt.append(t_gc)
                t_cols = colt[-1]
                prev["acts"] = t_cols
                f_toks = []
                for fc in range(2):
                    base_col = (fc * TT + cc_ * CK)
                    pcols = slice(fc * 128, fc * 128 + CK)
                    t_ep = p.act("activation", t1f[:, 0:CK],
                                 cl_sb[:, base_col:base_col + CK], AF.Exp,
                                 bias=cols1[:, fc:fc + 1],
                                 waits=[t_cols, prev.get("f_t1")])
                    t_en = p.act("activation", t2f[:, 0:CK],
                                 cl_sb[:, base_col:base_col + CK], AF.Exp,
                                 scale=c.cc[:, CC_NONE:CC_NONE + 1],
                                 bias=cl_sb[:, base_col + 31:base_col + 32],
                                 waits=[t_ep, prev.get("f_t2"), c.cc_tok])
                    t_ep2 = p.act("activation", qpre[:, 0:CK],
                                  cl_sb[:, base_col:base_col + CK], AF.Exp,
                                  bias=cols1[:, 2 + fc:3 + fc],
                                  waits=[t_en, prev.get("f_qp")])
                    mmbf = p.mm(pb[5][:, 0:CK], f2c_sb[:, fc * 128:(fc + 1) * 128],
                                beta_sb[0:4, ch], True, True,
                                waits=[t_be, t_f2c, prev.get("pbf")])
                    t_q = p.dve("tensor_tensor", qt_pad[:, pcols],
                                qn_bf[:, fc * TT:][:, ch], t1f[:, 0:CK],
                                ALU.mult, waits=[t_ep, qn_tok,
                                                 prev.get("qt_use")])
                    t_kh = p.dve("tensor_tensor", kh_pad[:, pcols],
                                 kn_bf[:, fc * TT:][:, ch], t1f[:, 0:CK],
                                 ALU.mult, waits=[t_ep, prev.get("kh_use")])
                    prev["f_t1"] = t_kh
                    t_kt0 = p.dve("tensor_tensor", t2f[:, CK:2 * CK],
                                  kn_bf[:, fc * TT:][:, ch], t2f[:, 0:CK],
                                  ALU.mult, waits=[t_en])
                    prev["f_t2"] = t_kt0
                    t_kt = p.dve("tensor_tensor", kt_pad[:, pcols],
                                 t2f[:, CK:2 * CK], pb[5][:, 0:CK], ALU.mult,
                                 waits=[t_kt0, mmbf, prev.get("kt_use")])
                    prev["pbf"] = t_kt
                    t_kc = p.dve("tensor_tensor", kc_pad[:, pcols],
                                 kn_bf[:, fc * TT:][:, ch], qpre[:, 0:CK],
                                 ALU.mult, waits=[t_ep2, prev.get("kc_use")])
                    prev["f_qp"] = t_kc
                    t_k2 = p.dve("tensor_scalar_mul", k2_fm[:, pcols],
                                 kt_pad[:, pcols], cols1[:, 4 + fc:5 + fc],
                                 waits=[t_kt, t_cols, prev.get("k2f_use")])
                    f_toks.append((t_q, t_kh, t_kt, t_kc, t_k2))
                t_fac = f_toks[-1][4]
                t_v1 = p.dve("tensor_scalar_mul", kt_ev[:], kt_pad[:], evcol[:],
                             waits=[t_fac, prev.get("ktv_use")])
                t_v2 = p.dve("tensor_scalar_mul", kt_od[:], kt_pad[:], odcol[:],
                             waits=[t_v1])
                t_v3 = p.dve("tensor_scalar_mul", kc_ev[:], kc_pad[:], evcol[:],
                             waits=[t_v2, prev.get("kcv_use")])
                t_v4 = p.dve("tensor_scalar_mul", kc_od[:], kc_pad[:], odcol[:],
                             waits=[t_v3])
                tk2 = None
                for fc in range(2):
                    tk2 = p.tr(pbt[:, fc * 128:(fc + 1) * 128],
                               k2_fm[:, fc * 128:(fc + 1) * 128],
                               identb[:], waits=[t_fac, t_idb]
                               + ([prev.get("k2t_use")] if fc == 0 else []))
                t_k2t = p.dve("tensor_copy", k2_tok[:], pbt[:, 0:256],
                              waits=[tk2, prev.get("k2t_use2")])
                tgt = p.tr(pb[5][0:64, 448:452], g_sb[0:4, ch],
                           c.ident[0:4, 0:4],
                           waits=[t_g, prev.get("gt_use")])
                t_gt = p.dve("tensor_copy", gt_c[:], pb[5][0:64, 448:452],
                             waits=[tgt, prev.get("gt_use2")])
                mmA = None
                for h in range(4):
                    fc = h // 2
                    qv = qt_pad
                    w = [t_v2, f_toks[fc][0]] if h == 0 else []
                    if h == 0:
                        w += [prev.get("at_use"), prev.get("papp")]
                    mmA = p.mm(pb[2][:, h * 128:(h + 1) * 128],
                               (kt_ev if h % 2 == 0 else kt_od)[:, fc * 128:(fc + 1) * 128],
                               qt_pad[:, fc * 128:(fc + 1) * 128],
                               True, True, waits=w)
                t_at = p.dve("tensor_tensor", at_sb[:], pb[2][:, :], amask[:],
                             ALU.mult, waits=[mmA, t_amc])
                prev["at_use"] = t_at
                mmL = None
                for h in range(4):
                    fc = h // 2
                    w = [t_v2, f_toks[fc][3]] if h == 0 else []
                    if h == 0:
                        w += [prev.get("l_use")]
                    mmL = p.mm(pb[3][:, h * 128:(h + 1) * 128],
                               kc_pad[:, fc * 128:(fc + 1) * 128],
                               (kt_ev if h % 2 == 0 else kt_od)[:, fc * 128:(fc + 1) * 128],
                               True, True, waits=w)
                t_l = p.dve("tensor_tensor", l_sb[:], pb[3][:, :], lmask[:],
                            ALU.mult, waits=[mmL, t_lmc])
                mmU = None
                for h in range(4):
                    fc = h // 2
                    w = [t_v2] if h == 0 else []
                    if h == 0:
                        w += [prev.get("u_use")]
                    mmU = p.mm(pb[4][:, h * 128:(h + 1) * 128],
                               (kt_ev if h % 2 == 0 else kt_od)[:, fc * 128:(fc + 1) * 128],
                               kc_pad[:, fc * 128:(fc + 1) * 128],
                               True, True, waits=w)
                t_u = p.dve("tensor_tensor", upow[0][:], pb[4][:, :], umask[:],
                            ALU.mult, waits=[mmU, t_umc, prev.get("u1_use")])
                lcur, lnext = l_sb, l2_sb
                t_lcur = t_l
                t_ucur = t_u
                for idx in range(5):
                    mmL2 = None
                    if idx < 4:
                        for h in range(4):
                            w = [t_lcur, t_ucur] if h == 0 else []
                            if h == 0:
                                w += [prev.get(f"lsq{idx % 2}")]
                            mmL2 = p.mm(pb[3][:, h * 128:(h + 1) * 128],
                                        upow[idx][:, h * 128:(h + 1) * 128],
                                        lcur[:, h * 128:(h + 1) * 128],
                                        True, True, waits=w)
                    mmU2 = None
                    for h in range(4):
                        w = [t_lcur, t_ucur] if h == 0 else []
                        if h == 0:
                            w += [prev.get("usq")]
                        mmU2 = p.mm(pb[4][:, h * 128:(h + 1) * 128],
                                    lcur[:, h * 128:(h + 1) * 128],
                                    upow[idx][:, h * 128:(h + 1) * 128],
                                    True, True, waits=w)
                    t_un = p.dve("tensor_copy", upow[idx + 1][:], pb[4][:, :],
                                 waits=[mmU2, prev.get(f"up{idx + 1}_use")])
                    prev["usq"] = t_un
                    t_ucur = t_un
                    if idx < 4:
                        t_ln2 = p.act("activation", lnext[:], pb[3][:, :],
                                      AF.Copy, waits=[mmL2,
                                                      prev.get(f"lsqe{idx % 2}")])
                        prev[f"lsq{idx % 2}"] = t_ln2
                        prev[f"lsqe{idx % 2}"] = t_ln2
                        t_lcur = t_ln2
                        lcur, lnext = lnext, lcur
                mmo1 = None
                for h in range(4):
                    w = [t_at, v_toks[cc_]] if h == 0 else []
                    if h == 0:
                        w += [prev.get("po_use")]
                    mmo1 = p.mm(pbO[:, h * 64:(h + 1) * 64],
                                at_sb[:, h * 128:(h + 1) * 128],
                                v_tile[:, cc_ * 256 + h * 64:
                                       cc_ * 256 + (h + 1) * 64],
                                True, True, waits=w)
                t_oin = p.dve("tensor_copy", oin[:], pbO[0:64, 0:256],
                              waits=[mmo1, prev.get("oin_use")])
                mmB = None
                for h in range(4):
                    fc = h // 2
                    w = [s_tok, f_toks[fc][3], t_v4] if h == 0 else []
                    if h == 0:
                        w += [prev.get("pb6_use")]
                    mmB = p.mm(pb[5][:, 64 + h * 64:64 + (h + 1) * 64],
                               (kc_ev if h % 2 == 0 else kc_od)[:, fc * 128:(fc + 1) * 128],
                               s_bf[:, (h // 2) * 64:(h // 2) * 64 + 64],
                               True, True, waits=w)
                t_b0 = p.dve("tensor_tensor", tw[0][:],
                             v_tile[:, cc_ * 256:(cc_ + 1) * 256],
                             pb[5][:, 64:320], ALU.subtract,
                             waits=[mmB, v_toks[cc_], prev.get("tw0_use")])
                prev["pb6_use"] = t_b0
                tcur = 0
                t_tw = t_b0
                for ai, ui in enumerate([5, 4, 3, 2, 1, 0]):
                    mmap = None
                    for h in range(4):
                        w = [t_tw, prev.get("usq") if ai == 0 else None] if h == 0 else []
                        if h == 0:
                            w += [prev.get("papp")]
                        mmap = p.mm(pb[2][:, h * 64:(h + 1) * 64],
                                    upow[ui][:, h * 128:(h + 1) * 128],
                                    tw[tcur][:, h * 64:(h + 1) * 64],
                                    True, True, waits=w)
                    if ai < 5:
                        t_tw = p.dve("tensor_tensor", tw[1 - tcur][:],
                                     pb[2][:, 0:256], tw[tcur][:], ALU.add,
                                     waits=[mmap, prev.get("tw_use%d" % (1 - tcur))])
                        prev["tw_use%d" % tcur] = t_tw
                        tcur = 1 - tcur
                    else:
                        t_tw = p.dve("tensor_tensor", w_sb[:], tw[tcur][:],
                                     pb[2][:, 0:256], ALU.subtract,
                                     waits=[mmap, prev.get("w_use")])
                        prev["tw_use%d" % tcur] = t_tw
                    prev["papp"] = t_tw
                t_w = t_tw
                mmo2 = None
                for h in range(4):
                    fc = h // 2
                    w = [s_tok, qn_tok, t_oin] if h == 0 else []
                    mmo2 = p.mm(pbO[0:64, h * 64:(h + 1) * 64],
                                (qn_ev if h % 2 == 0 else qn_od)[:, fc * TT:][:, ch],
                                s_bf[:, (h // 2) * 64:(h // 2) * 64 + 64],
                                True, True, waits=w)
                mmP = None
                for h in range(4):
                    w = [t_k2t, t_w] if h == 0 else []
                    if h == 0:
                        w += [prev.get("pP_use")]
                    mmP = p.mm(pb[5][(h % 2) * 64:(h % 2) * 64 + 64,
                                     320 + (h // 2) * 64:320 + (h // 2) * 64 + 64],
                               k2_tok[:, (h // 2) * 128 + (h % 2) * 64:
                                      (h // 2) * 128 + (h % 2) * 64 + 64],
                               w_sb[:, h * 64:(h + 1) * 64],
                               True, True, waits=w)
                t_s = None
                for fc in range(2):
                    t_s1 = p.dve("tensor_scalar_mul",
                                 s_f[:, fc * 64:(fc + 1) * 64],
                                 s_f[:, fc * 64:(fc + 1) * 64],
                                 cols1[:, 6 + fc:7 + fc],
                                 waits=[mmP, t_cols, mmB])
                    t_s = p.dve("scalar_tensor_tensor",
                                s_f[:, fc * 64:(fc + 1) * 64],
                                pb[5][:, 320 + fc * 64:320 + (fc + 1) * 64], 1.0,
                                s_f[:, fc * 64:(fc + 1) * 64], ALU.mult,
                                ALU.add, waits=[t_s1])
                prev["pP_use"] = t_s
                s_tok = p.dve("tensor_copy", s_bf[:], s_f[:],
                              waits=[t_s, mmo2])
                t_o1 = p.dve("tensor_tensor", osc[:, 0:256], pb[5][0:64, 0:256],
                             oin[0:64, 0:256] if False else pb[5][0:64, 0:256],
                             ALU.bypass, waits=[]) if False else None
                t_sum = p.dve("tensor_tensor", osum[:], oin[:],
                              pbO[0:64, 0:256], ALU.add,
                              waits=[mmo2, t_oin, prev.get("osum_use")])
                t_osq = p.act("activation", osc[:, 0:256], osum[:],
                              AF.Square, waits=[t_sum, prev.get("osc_use")])
                t_ssq = p.dve("tensor_reduce", ssq4[:, 0:4],
                              osc[:, 0:256].rearrange("p (h e) -> p h e", e=64),
                              AX.X, ALU.add, waits=[t_osq])
                t_var = p.dve("tensor_scalar", ssq4[:, 4:8], ssq4[:, 0:4],
                              1.0 / 64.0, 1e-6, ALU.mult, ALU.add,
                              waits=[t_ssq])
                t_sqr = p.act("activation", ssq4[:, 4:8], ssq4[:, 4:8], AF.Sqrt,
                              waits=[t_var])
                t_rc = p.dve("reciprocal", ssq4[:, 8:12], ssq4[:, 4:8],
                             waits=[t_sqr])
                t_cg = p.dve("tensor_tensor", gt_c[:], gt_c[:], ssq4[:, 8:12],
                             ALU.mult, waits=[t_rc, t_gt])
                t_of = p.dve("tensor_tensor",
                             osc[:, 0:256].rearrange("p (h e) -> p h e", e=64),
                             osum[:].rearrange("p (h e) -> p h e", e=64),
                             gt_c[:].rearrange("p (h o) -> p h o", o=1)
                             .broadcast_to([64, 4, 64]), ALU.mult,
                             waits=[t_cg])
                t_o2 = p.dve("tensor_tensor", osum[:], osc[:, 0:256],
                             rms_sb[0:64, 0:256], ALU.mult,
                             waits=[t_of, ld0[1]])
                prev["po_use"] = t_sum
                prev["oin_use"] = t_sum
                prev["osc_use"] = t_o2
                st = p.dma("sync", o_d[ti * TT + cc_ * CK:
                                       ti * TT + (cc_ + 1) * CK, :],
                           osum[:], waits=[t_o2])
                prev["gt_use"] = t_o2
                prev["gt_use2"] = t_o2
                prev["osum_use"] = st
                prev["osb_st"] = st
                last_store = st
                prev["cols_use"] = t_s
                prev["pb5_war"] = t_s
                prev["qt_use"] = mmo2
                prev["kh_use"] = mmL
                prev["kt_use"] = mmU
                prev["kc_use"] = mmB
                prev["ktv_use"] = mmU
                prev["kcv_use"] = mmB
                prev["k2f_use"] = tk2
                prev["k2t_use"] = t_k2t
                prev["k2t_use2"] = mmP
                prev["l_use"] = prev.get("lsq0")
                prev["u_use"] = t_un
                prev["u1_use"] = mmap
                prev["w_use"] = mmP
                prev["tw0_use"] = t_tw
                prev["v_use"] = mmB
                prev["beta_use"] = t_kt
                prev["g_use"] = t_gt
                prev["qn_use"] = mmo2

        p.barrier_end([last_store])
        nc_final = p.finalize()
    return nc_final


def prep_attn1_inputs(w, heads):
    h0 = heads[0]
    colsl = slice(h0 * 64, (h0 + 4) * 64)
    wq = _f32(w["Wq"].T[:, colsl]); wk = _f32(w["Wk"].T[:, colsl])
    wv = _f32(w["Wv"].T[:, colsl])
    wcd = np.zeros((3 * DC * 4 * 128, 128), np.float32)
    for icv, cw in enumerate((w["wq_conv"], w["wk_conv"], w["wv_conv"])):
        cw2 = np.asarray(cw)[:, 0, :]
        for dc in range(DC):
            for tau in range(4):
                blk = (icv * DC + dc) * 4 + tau
                wcd[blk * 128:(blk + 1) * 128, :] = np.diag(
                    cw2[dc * 128:(dc + 1) * 128, tau])
    bconv = np.zeros((128, 3 * DC), np.float32)
    for icv, cb in enumerate((w["bq_conv"], w["bk_conv"], w["bv_conv"])):
        bconv[:, icv * DC:(icv + 1) * DC] = np.asarray(cb).reshape(DC, 128).T
    amask = np.zeros((128, 512), np.float32)
    am = np.tril(np.ones((CK, CK), np.float32)).T
    lm = np.zeros((128, 512), np.float32)
    lmm = np.tril(np.ones((CK, CK), np.float32), -1)
    um = np.zeros((128, 512), np.float32)
    umm = lmm.T
    for h in range(4):
        amask[0:CK, h * 128:h * 128 + CK] = am
        lm[0:CK, h * 128:h * 128 + CK] = lmm
        um[0:CK, h * 128:h * 128 + CK] = umm
    e2c = np.zeros((128, 2), np.float32)
    e2c[0:64, 0] = 1.0
    e2c[64:128, 1] = 1.0
    e2t = e2c.T.copy()
    f2c = np.zeros((4, 2 * 128), np.float32)
    for fc in range(2):
        for hh in range(2):
            f2c[fc * 2 + hh, fc * 128 + hh * 64:fc * 128 + (hh + 1) * 64] = 1.0
    rmsfull = np.tile(np.asarray(w["rms_w"])[h0:h0 + 4].reshape(1, 256),
                      (128, 1)).astype(np.float32)
    return dict(
        wcdiag=wcd, bconv=bconv, WqT=wq, WkT=wk, WvT=wv,
        bq2=_f32(np.asarray(w["bq"])[colsl].reshape(2, 128).T),
        bk2=_f32(np.asarray(w["bk"])[colsl].reshape(2, 128).T),
        bvrow=_f32(np.asarray(w["bv"])[colsl].reshape(1, 256)),
        WadT=_f32(w["Wad"].T), bad=_f32(np.asarray(w["bad"]).reshape(64, 1)),
        WauT=_f32(w["Wau"].T[:, colsl]),
        bau2=_f32(np.asarray(w["bau"])[colsl].reshape(2, 128).T),
        WbeT=_f32(w["Wbeta"].T[:, h0:h0 + 4]),
        bbe=_f32(np.asarray(w["bbeta"])[h0:h0 + 4].reshape(4, 1)),
        WgdT=_f32(w["Wgd"].T), bgd=_f32(np.asarray(w["bgd"]).reshape(64, 1)),
        WguT=_f32(w["Wgu"].T[:, h0:h0 + 4]),
        bgu=_f32(np.asarray(w["bgu"])[h0:h0 + 4].reshape(4, 1)),
        rmsfull=rmsfull, amask=amask, lmask=lm, umask=um,
        e2c=e2c, e2t=e2t, f2c=f2c, ident=np.eye(128, dtype=np.float32))


def run_attn1(resids_full, weights, tokq=16384):
    nc = build_attn1(tokq)
    in_maps = []
    for cc_ in range(NCORES):
        b, hq = cc_ // 4, (cc_ % 4) * 4
        wts = prep_attn1_inputs(weights, [hq])
        in_maps.append(dict(x=resids_full[b], **wts))
    outs = _run(nc, in_maps, ["o_heads"])
    return [o["o_heads"] for o in outs]



def build_ffn1(half=0.5, tok=TOK):
    NTt = tok // TT
    nc = bass.Bass("TRN2", target_bir_lowering=False, debug=False)
    x_d = nc.dram_tensor("x", [D, tok], F32, kind="ExternalInput").ap()
    gw_d = nc.dram_tensor("gate_w", [D, 8], F32, kind="ExternalInput").ap()
    shg_d = nc.dram_tensor("sh_gate", [D, 2048], F32, kind="ExternalInput").ap()
    shu_d = nc.dram_tensor("sh_up", [D, 2048], F32, kind="ExternalInput").ap()
    shd_d = nc.dram_tensor("sh_down", [2048, D], F32, kind="ExternalInput").ap()
    rtg_d = nc.dram_tensor("rt_gate", [8, D, 2048], F32, kind="ExternalInput").ap()
    rtu_d = nc.dram_tensor("rt_up", [8, D, 2048], F32, kind="ExternalInput").ap()
    rtd_d = nc.dram_tensor("rt_down", [8, 2048, D], F32, kind="ExternalInput").ap()
    esel_d = nc.dram_tensor("esel", [8, D], F32, kind="ExternalInput").ap()
    ident_d = nc.dram_tensor("ident", [128, 128], F32, kind="ExternalInput").ap()
    out_d = nc.dram_tensor("resid_out", [D, tok], F32,
                           kind="ExternalOutput").ap()

    xn_d = nc.dram_tensor("xn_scr", [D, tok], BF16).ap()
    xs_d = nc.dram_tensor("xs_scr", [D, tok], BF16).ap()
    ohs_d = nc.dram_tensor("ohs_scr", [8, tok], F32).ap()
    y_d = nc.dram_tensor("y_scr", [D, tok], F32).ap()

    p = P(nc)
    with contextlib.ExitStack() as es:
        c = alloc_common(nc, es, TT)
        sb = lambda name, shape, dt: es.enter_context(
            nc.sbuf_tensor(name, shape, dt))
        pb = [es.enter_context(nc.psum_tensor(f"pb{i}", [128, TT], F32))
              for i in range(8)]
        gw_bf = sb("gw_bf", [128, 64], BF16)
        lem = sb("lem", [8, TT], F32)
        ltok = sb("ltok", [128, 32], F32)
        rmax = sb("rmax", [128, 4], F32)
        score = sb("score", [128, 4], F32)
        oh = sb("oh", [128, 32], F32)
        ohs_em = sb("ohs_em", [8, TT], F32)
        esel_sb = sb("esel_sb", [8, D], F32)
        srow = sb("srow", [1, TT], F32)
        scf = sb("scf", [128, TT], F32)
        wg_bf = sb("wg_bf", [128, DC * 2048], BF16)
        wu_bf = sb("wu_bf", [128, DC * 2048], BF16)
        wd_bf = sb("wd_bf", [128, 16 * D], BF16)
        wstage = sb("wstage", [128, 1024], F32)
        xnb = [sb(f"xnb{i}", [128, DC * TT], BF16) for i in range(2)]
        ohs_l = sb("ohs_l", [8, TT], F32)
        mask_bf = sb("mask_bf", [128, TT], BF16)
        act_sb = sb("act_sb", [128, 16 * TT], BF16)
        gstage = [sb(f"gstage{i}", [128, TT], F32) for i in range(2)]
        tstage = [sb(f"tstage{i}", [128, TT], BF16) for i in range(2)]
        ystage = [sb(f"ystage{i}", [128, TT], F32) for i in range(2)]
        ybst = [sb("ybst0", [128, TT], F32)]
        resout = [sb("resout0", [128, TT], F32)]

        ct = emit_const_init(p, c, ident_d)
        t_esel = p.dma("sync", esel_sb[:], esel_d)
        t_gw = p.dma("sync",
                     wstage[:, 0:64].rearrange("p (c e) -> p c e", e=8),
                     gw_d.rearrange("(c p) e -> p c e", p=128))
        t_gwc = p.dve("tensor_copy", gw_bf[:], wstage[:, 0:64], waits=[t_gw])

        last_xn_store = None
        last_xs_store = None
        last_ohs_store = None
        prev = {}
        for ti in range(NTt):
            cols = slice(ti * TT, (ti + 1) * TT)
            ln = emit_ln(p, c, pb, x_d, cols,
                         xt_war=[prev.get("xn"), prev.get("xs_store")],
                         xn_war=[prev.get("xn_store")],
                         pb_war=[prev.get("route_done")])
            st = None
            for dc in range(DC):
                st = p.dma("sync", xn_d[dc * 128:(dc + 1) * 128, cols],
                           c.xn[:, dc * TT:(dc + 1) * TT], waits=[ln["xn"]])
            last_xn_store = st
            mmr = None
            for dc in range(DC):
                w = []
                if dc == 0:
                    w = [ln["xn"], t_gwc, prev.get("route_done")]
                mmr = p.mm(pb[4][0:8, :], gw_bf[:, dc * 8:(dc + 1) * 8],
                           c.xn[:, dc * TT:(dc + 1) * TT], dc == 0,
                           dc == DC - 1, waits=w)
            t_lem = p.dve("tensor_copy", lem[:], pb[4][0:8, :], waits=[mmr])
            tt0 = None
            for g in range(4):
                w = [t_lem, ct[4]]
                if g == 0:
                    w.append(prev.get("route_done"))
                tt0 = p.tr(pb[5][:, g * 8:(g + 1) * 8],
                           lem[0:8, g * 128:(g + 1) * 128],
                           c.ident[0:8, 0:8], waits=w)
            t_ltok = p.dve("tensor_copy", ltok[:], pb[5][:, 0:32], waits=[tt0])
            t_rmax = p.dve("tensor_reduce", rmax[:],
                           ltok[:].rearrange("p (g e) -> p g e", e=8), AX.X,
                           ALU.max, waits=[t_ltok])
            t_sc = p.act("activation", score[:], rmax[:], AF.Sigmoid,
                         waits=[t_rmax])
            t_oh = p.dve("tensor_tensor",
                         oh[:].rearrange("p (g e) -> p g e", e=8),
                         ltok[:].rearrange("p (g e) -> p g e", e=8),
                         rmax[:].rearrange("p (g o) -> p g o", o=1)
                         .broadcast_to([128, 4, 8]), ALU.is_ge,
                         waits=[t_rmax])
            tt1 = None
            for g in range(4):
                tt1 = p.tr(pb[6][0:8, g * 128:(g + 1) * 128],
                           oh[:, g * 8:(g + 1) * 8], c.ident[:],
                           waits=[t_oh])
            t_em = p.dve("tensor_copy", ohs_em[:], pb[6][0:8, 0:TT],
                         waits=[tt1, last_ohs_store])
            last_ohs_store = p.dma("sync", ohs_d[0:8, cols], ohs_em[:],
                                   waits=[t_em])
            tt2 = None
            for g in range(4):
                tt2 = p.mm(pb[7][0:1, g * 128:(g + 1) * 128],
                           score[:, g:g + 1], c.ident[:], True, True,
                           waits=[t_sc] + ([prev.get("xs_done")] if g == 0 else []))
            t_srow = p.dve("tensor_copy", srow[:], pb[7][0:1, 0:TT],
                           waits=[tt2])
            t_scmm = p.mm(pb[7][:, :], c.onesrow_f[:], srow[:], True, True,
                          waits=[t_srow])
            t_scf = p.dve("tensor_copy", scf[:], pb[7][:, :], waits=[t_scmm])
            xs_sb = c.xsq
            xs_tok = None
            for dc in range(DC):
                s = slice(dc * TT, (dc + 1) * TT)
                xs_tok = p.dve("tensor_tensor", xs_sb[:, s], c.xn[:, s],
                               scf[:], ALU.mult,
                               waits=[t_scf, ln["xn"]]
                               + ([prev.get("xs_store")] if dc == 0 else []))
            stx = None
            for dc in range(DC):
                stx = p.dma("sync", xs_d[dc * 128:(dc + 1) * 128, cols],
                            xs_sb[:, dc * TT:(dc + 1) * TT], waits=[xs_tok])
            last_xs_store = stx
            prev = {"xn": xs_tok, "xn_store": st, "route_done": t_em,
                    "xs_done": t_scf, "xs_store": stx}

        def load_gu(e, war_tok, prev_tok):
            piece = []
            pr = prev_tok
            for which, buf in ((0, wg_bf), (1, wu_bf)):
                for dc in range(DC):
                    pr2 = []
                    for hh in range(2):
                        if e == 0:
                            src_ap = (shg_d if which == 0 else shu_d)[
                                dc * 128:(dc + 1) * 128,
                                hh * 1024:(hh + 1) * 1024]
                        else:
                            src_ap = (rtg_d if which == 0 else rtu_d)[
                                e - 1, dc * 128:(dc + 1) * 128,
                                hh * 1024:(hh + 1) * 1024]
                        t_d = p.dma("scalar", wstage[:], src_ap,
                                    waits=[pr, war_tok])
                        pr = p.act("activation",
                                   buf[:, dc * 2048 + hh * 1024:
                                       dc * 2048 + (hh + 1) * 1024],
                                   wstage[:], AF.Copy, waits=[t_d, war_tok])
                    piece.append(pr)
            return piece

        def load_wd(e, war_tok, prev_tok):
            piece = []
            pr = prev_tok
            for ic in range(16):
                if e == 0:
                    src_ap = shd_d[ic * 128:(ic + 1) * 128, :]
                else:
                    src_ap = rtd_d[e - 1, ic * 128:(ic + 1) * 128, :]
                t_d = p.dma("scalar", wstage[:], src_ap,
                            waits=[pr, war_tok])
                pr = p.dve("tensor_copy", wd_bf[:, ic * 1024:(ic + 1) * 1024],
                           wstage[:], waits=[t_d, war_tok])
                piece.append(pr)
            return piece

        gu_toks = load_gu(0, None, t_gwc)
        wd_toks = load_wd(0, None, gu_toks[-1])
        xnb_war = [None, None]
        mask_war = None
        pg_war = [None, None]
        pu_war = [None, None]
        py_war = [None, None]
        yst_war = [None, None]
        gst_war = [None, None]
        tst_war = [None, None]
        last_y_dma = None
        act_end = None
        y_end = None
        for e in range(9):
            routed = e > 0
            for ti in range(NTt):
                cols = slice(ti * TT, (ti + 1) * TT)
                bi = ti % 2
                src_d = xs_d if routed else xn_d
                t_xn = p.dma("sync",
                             xnb[bi][:].rearrange("p (c t) -> p c t", t=TT),
                             src_d[:, cols].rearrange("(c p) t -> p c t",
                                                      p=128),
                             waits=[xnb_war[bi], last_xn_store,
                                    last_xs_store])
                t_mask = None
                if routed:
                    t_ohl = p.dma("sync", ohs_l[:], ohs_d[0:8, cols],
                                  waits=[last_ohs_store, mask_war])
                    t_mm_m = p.mm(pb[6][:, :],
                                  esel_sb[:, (e - 1) * 128:e * 128],
                                  ohs_l[:], True, True,
                                  waits=[t_ohl, mask_war, t_esel])
                    t_mask = p.dve("tensor_copy", mask_bf[:], pb[6][:, :],
                                   waits=[t_mm_m])
                    mask_war = t_mask
                t_acts = []
                for ic in range(16):
                    pg = pb[ic % 2]
                    pu = pb[2 + ic % 2]
                    mmg = mmu = None
                    for dcc in range(DC):
                        w = []
                        if dcc == 0:
                            w = [pg_war[ic % 2], t_xn]
                        if ti == 0:
                            w = list(w) + [gu_toks[dcc]]
                        mmg = p.mm(pg[:, :],
                                   wg_bf[:, dcc * 2048 + ic * 128:
                                         dcc * 2048 + (ic + 1) * 128],
                                   xnb[bi][:, dcc * TT:(dcc + 1) * TT],
                                   dcc == 0, dcc == DC - 1, waits=w)
                        w2_ = []
                        if dcc == 0:
                            w2_ = [pu_war[ic % 2]]
                        if ti == 0:
                            w2_ = list(w2_) + [gu_toks[8 + dcc]]
                        mmu = p.mm(pu[:, :],
                                   wu_bf[:, dcc * 2048 + ic * 128:
                                         dcc * 2048 + (ic + 1) * 128],
                                   xnb[bi][:, dcc * TT:(dcc + 1) * TT],
                                   dcc == 0, dcc == DC - 1, waits=w2_)
                    t_si = p.act("activation", gstage[ic % 2][:], pg[:, :],
                                 AF.Silu, waits=[mmg, gst_war[ic % 2]])
                    pg_war[ic % 2] = t_si
                    if routed:
                        t_t = p.dve("tensor_tensor", tstage[ic % 2][:],
                                    gstage[ic % 2][:], pu[:, :], ALU.mult,
                                    waits=[t_si, mmu, tst_war[ic % 2]])
                        pu_war[ic % 2] = t_t
                        gst_war[ic % 2] = t_t
                        t_a = p.dve("tensor_tensor",
                                    act_sb[:, ic * TT:(ic + 1) * TT],
                                    tstage[ic % 2][:], mask_bf[:], ALU.mult,
                                    waits=[t_t, t_mask])
                        tst_war[ic % 2] = t_a
                    else:
                        t_a = p.dve("tensor_tensor",
                                    act_sb[:, ic * TT:(ic + 1) * TT],
                                    gstage[ic % 2][:], pu[:, :], ALU.mult,
                                    waits=[t_si, mmu])
                        pu_war[ic % 2] = t_a
                        gst_war[ic % 2] = t_a
                    t_acts.append(t_a)
                act_end = t_acts[-1]
                xnb_war[bi] = act_end
                for dcg in range(DC):
                    py = pb[4 + dcg % 2]
                    mmy = None
                    for ic in range(16):
                        w = []
                        if ic == 0:
                            w = [py_war[dcg % 2], t_acts[-1]]
                        if ti == 0:
                            w = list(w) + [wd_toks[ic]]
                        mmy = p.mm(py[:, :],
                                   wd_bf[:, ic * 1024 + dcg * 128:
                                         ic * 1024 + (dcg + 1) * 128],
                                   act_sb[:, ic * TT:(ic + 1) * TT],
                                   ic == 0, ic == 15, waits=w)
                    y_end = mmy
                    t_ev = p.act("activation", ystage[dcg % 2][:], py[:, :],
                                 AF.Copy, waits=[mmy, yst_war[dcg % 2]])
                    py_war[dcg % 2] = t_ev
                    kw = {} if e == 0 else {"accum_op": ALU.add}
                    t_yd = p.dma("gpsimd",
                                 y_d[dcg * 128:(dcg + 1) * 128, cols],
                                 ystage[dcg % 2][:],
                                 waits=[t_ev,
                                        ("sgp", p.cnt["sgp"], "gpsimd")], **kw)
                    yst_war[dcg % 2] = t_yd
                    last_y_dma = t_yd
            if e < 8:
                gu_toks = load_gu(e + 1, act_end, last_y_dma)
                wd_toks = load_wd(e + 1, y_end, gu_toks[-1])

        prev3 = {}
        last_store = None
        for ti in range(NTt):
            cols = slice(ti * TT, (ti + 1) * TT)
            for dc in range(DC):
                t_lx = p.dma("sync", c.xt[:, dc * TT:(dc + 1) * TT],
                             x_d[dc * 128:(dc + 1) * 128, cols],
                             waits=[prev3.get("ro"), act_end])
                t_ly = p.dma("sync", ybst[0][:],
                             y_d[dc * 128:(dc + 1) * 128, cols],
                             waits=[last_y_dma, prev3.get("yb")])
                t_ro = p.dve("scalar_tensor_tensor", resout[0][:],
                             ybst[0][:], half,
                             c.xt[:, dc * TT:(dc + 1) * TT], ALU.mult,
                             ALU.add, waits=[t_lx, t_ly, prev3.get("rost")])
                prev3["yb"] = t_ro
                t_st = p.dma("sync", out_d[dc * 128:(dc + 1) * 128, cols],
                             resout[0][:], waits=[t_ro])
                prev3["rost"] = t_st
                prev3["ro"] = t_st
                last_store = t_st

        p.barrier_end([last_store])
        nc_final = p.finalize()
    return nc_final


def prep_ffn1_inputs(gate_w, sh_gate, sh_up, sh_down, rt_gate, rt_up, rt_down):
    esel = np.zeros((8, D), np.float32)
    for e in range(8):
        esel[e, e * 128:(e + 1) * 128] = 1.0
    return dict(gate_w=_f32(gate_w), sh_gate=_f32(sh_gate), sh_up=_f32(sh_up),
                sh_down=_f32(sh_down), rt_gate=_f32(rt_gate),
                rt_up=_f32(rt_up), rt_down=_f32(rt_down), esel=esel,
                ident=np.eye(128, dtype=np.float32))


def run_ffn1(resids, wts, half=0.5, tok=TOK):
    nc = build_ffn1(half, tok)
    in_maps = [dict(x=_f32(r), **wts) for r in resids]
    outs = _run(nc, in_maps, ["resid_out"])
    return [o["resid_out"] for o in outs]



def build_womerge(with_bias, tok=TOK):
    NTt = tok // TT
    nc = bass.Bass("TRN2", target_bir_lowering=False, debug=False)
    x_d = nc.dram_tensor("x", [D, tok], F32, kind="ExternalInput").ap()
    o_d = nc.dram_tensor("ofm", [D, tok], F32, kind="ExternalInput").ap()
    wo_d = nc.dram_tensor("WoT", [D, D], F32, kind="ExternalInput").ap()
    bo_d = nc.dram_tensor("bo8", [128, 8], F32, kind="ExternalInput").ap()
    out_d = nc.dram_tensor("resid_out", [D, tok], F32,
                           kind="ExternalOutput").ap()
    p = P(nc)
    with contextlib.ExitStack() as es:
        sb = lambda name, shape, dt: es.enter_context(
            nc.sbuf_tensor(name, shape, dt))
        pb = [es.enter_context(nc.psum_tensor(f"pb{i}", [128, TT], F32))
              for i in range(4)]
        wo_bf = sb("wo_bf", [128, DC * D], BF16)
        wstage = sb("wstage", [128, 2048], F32)
        bo_sb = sb("bo_sb", [128, 8], F32)
        xt = sb("xt", [128, TT], F32)
        ot = sb("ot", [128, DC * TT], BF16)
        of32 = sb("of32", [128, TT], F32)
        ro = [sb(f"ro{i}", [128, TT], F32) for i in range(2)]

        t_bo = p.dma("sync", bo_sb[:], bo_d)
        pr = t_bo
        for fcb in range(DC):
            for half in range(2):
                t_d = p.dma("scalar", wstage[:, 0:512],
                            wo_d[fcb * 128:(fcb + 1) * 128,
                                 half * 512:(half + 1) * 512], waits=[pr])
                pr = p.act("activation",
                           wo_bf[:, fcb * D + half * 512:
                                 fcb * D + (half + 1) * 512],
                           wstage[:, 0:512], AF.Copy, waits=[t_d])
        t_w = pr
        prev = {}
        last_store = None
        for ti in range(NTt):
            cols = slice(ti * TT, (ti + 1) * TT)
            ots = []
            t_c = None
            for fcb in range(DC):
                t_lo = p.dma("sync", of32[:],
                             o_d[fcb * 128:(fcb + 1) * 128, cols],
                             waits=[t_c if t_c is not None
                                    else prev.get("of_last")])
                t_c = p.dve("tensor_copy", ot[:, fcb * TT:(fcb + 1) * TT],
                            of32[:], waits=[t_lo, prev.get("ot_use")
                                            if fcb == 0 else None])
                ots.append(t_c)
            prev["of_last"] = t_c
            for dc in range(DC):
                mmw = None
                for fcb in range(DC):
                    w = [ots[fcb]]
                    if fcb == 0:
                        w += [t_w, prev.get(f"py{dc % 2}")]
                    mmw = p.mm(pb[dc % 2][:, 0:TT],
                               wo_bf[:, fcb * D + dc * 128:
                                     fcb * D + (dc + 1) * 128],
                               ot[:, fcb * TT:(fcb + 1) * TT],
                               fcb == 0, fcb == DC - 1, waits=w)
                t_lx = p.dma("sync", xt[:], x_d[dc * 128:(dc + 1) * 128, cols],
                             waits=[prev.get("xt_use")])
                if with_bias:
                    t_r = p.dve("scalar_tensor_tensor", ro[dc % 2][:],
                                pb[dc % 2][:, 0:TT], bo_sb[:, dc:dc + 1],
                                xt[:], ALU.add, ALU.add,
                                waits=[mmw, t_lx, t_bo,
                                       prev.get(f"ro{dc % 2}")])
                else:
                    t_r = p.dve("tensor_tensor", ro[dc % 2][:],
                                pb[dc % 2][:, 0:TT], xt[:], ALU.add,
                                waits=[mmw, t_lx, prev.get(f"ro{dc % 2}")])
                prev[f"py{dc % 2}"] = t_r
                prev["xt_use"] = t_r
                t_st = p.dma("sync", out_d[dc * 128:(dc + 1) * 128, cols],
                             ro[dc % 2][:], waits=[t_r])
                prev[f"ro{dc % 2}"] = t_st
                last_store = t_st
            prev["ot_use"] = mmw
        p.barrier_end([last_store])
        nc_final = p.finalize()
    return nc_final


def run_womerge(resids, ofms, WoT, bo8, with_bias, tok=TOK):
    nc = build_womerge(with_bias, tok)
    in_maps = [dict(x=_f32(resids[cc2]), ofm=_f32(ofms[cc2]), WoT=_f32(WoT),
                    bo8=_f32(bo8)) for cc2 in range(NCORES)]
    outs = _run(nc, in_maps, ["resid_out"])
    return [o["resid_out"] for o in outs]


def prep_ffn0_inputs(Wr, br, W1, b1, W2, b2):
    b1c = _f32(np.asarray(b1).reshape(8, 16, 128).transpose(2, 0, 1)
               .reshape(128, 128))
    esel = np.zeros((8, D), np.float32)
    for e in range(8):
        esel[e, e * 128:(e + 1) * 128] = 1.0
    return dict(Wr=_f32(Wr), br=_f32(np.asarray(br).reshape(1, 8)),
                W1=_f32(W1), b1c=b1c, W2=_f32(W2),
                B2=_f32(b2), esel=esel,
                ident=np.eye(128, dtype=np.float32))


def run_ffn0(resids, wts, half=0.5, tok=TOK):
    nc = build_ffn0(half, tok)
    in_maps = [dict(x=_f32(r), **wts) for r in resids]
    outs = _run(nc, in_maps, ["resid_out"])
    return [o["resid_out"] for o in outs]


def _cat_full(resids):
    return {b: np.ascontiguousarray(
        np.concatenate([resids[b * 4 + q] for q in range(4)], axis=1))
        for b in range(2)}


def _ofm_slices(o_heads):
    out = []
    for cc2 in range(NCORES):
        b, q = cc2 // 4, cc2 % 4
        ofb = np.concatenate([o_heads[b * 4 + i] for i in range(4)], axis=1)
        out.append(np.ascontiguousarray(ofb[q * 4096:(q + 1) * 4096, :].T))
    return out


def kernel(**inputs):
    x = np.asarray(inputs["L0_attn_x"], np.float32)
    resids = [np.ascontiguousarray(x[cc2 // 4,
                                     (cc2 % 4) * 4096:(cc2 % 4 + 1) * 4096,
                                     :].T) for cc2 in range(NCORES)]
    wts0 = prep_ffn0_inputs(inputs["L0_ffn_Wr"], inputs["L0_ffn_br"],
                            inputs["L0_ffn_W1"], inputs["L0_ffn_b1"],
                            inputs["L0_ffn_W2"], inputs["L0_ffn_b2"])
    wts1 = prep_ffn1_inputs(inputs["L1_ffn_gate_w"], inputs["L1_ffn_sh_gate"],
                            inputs["L1_ffn_sh_up"], inputs["L1_ffn_sh_down"],
                            inputs["L1_ffn_rt_gate"], inputs["L1_ffn_rt_up"],
                            inputs["L1_ffn_rt_down"])
    w1a = {k[len("L1_attn_"):]: np.asarray(v) for k, v in inputs.items()
           if k.startswith("L1_attn_")}
    zeros8 = np.zeros((128, 8), np.float32)
    bo8 = _f32(np.asarray(inputs["L1_attn_bo"]).reshape(8, 128).T)

    resids = run_ffn0(resids, wts0)
    o0 = run_attn0(_cat_full(resids),
                   dict(Wq=inputs["L0_attn_Wq"], Wk=inputs["L0_attn_Wk"],
                        Wv=inputs["L0_attn_Wv"], Wg=inputs["L0_attn_Wg"],
                        bg=inputs["L0_attn_bg"]))
    resids = run_womerge(resids, _ofm_slices(o0),
                         np.asarray(inputs["L0_attn_Wo"]).T, zeros8, False)
    resids = run_ffn0(resids, wts0)
    resids = run_ffn1(resids, wts1)
    o1 = run_attn1(_cat_full(resids), w1a)
    resids = run_womerge(resids, _ofm_slices(o1),
                         np.asarray(w1a["Wo"]).T, bo8, True)
    resids = run_ffn1(resids, wts1)

    out = np.empty((2, 16384, 1024), np.float32)
    for cc2 in range(NCORES):
        b, q = cc2 // 4, cc2 % 4
        out[b, q * 4096:(q + 1) * 4096, :] = resids[cc2].T
    return out
